# version 10
# baseline (speedup 1.0000x reference)
import contextlib
import numpy as np
import concourse.bass as bass
import concourse.mybir as mybir
from concourse.bass_utils import run_bass_kernel_spmd

F32 = mybir.dt.float32
BF16 = mybir.dt.bfloat16
AF = mybir.ActivationFunctionType
ALU = mybir.AluOpType
AX = mybir.AxisListType

FULL = dict(L=2048, H=4096, DIN=8192, NST=16, KC=4, R=256)

ENGINES = ("tensor", "vector", "scalar", "gpsimd", "sync")
EPS_LN = 1e-5
EPS_RMS = 1e-6
GROUPS = [[0, 1], [2, 3], [4, 5], [6, 7]]


class Sched:

    NLANES = 4

    def __init__(self):
        self.ops = []
        self.lastw = {}
        self.readers = {}
        self.n_cc = 0
        self.dma_count = {}
        self.last_lane = {}

    def add(self, eng, emit, reads=(), writes=(), inc=1, sem=None, tag=""):
        deps = set()
        for t in reads:
            w = self.lastw.get(t)
            if w is not None:
                deps.add(w)
        for t in writes:
            w = self.lastw.get(t)
            if w is not None:
                deps.add(w)
            deps.update(self.readers.get(t, ()))
        i = len(self.ops)
        deps.discard(i)
        if sem is None:
            if inc == 16:
                n = self.dma_count.get(eng, 0)
                self.dma_count[eng] = n + 1
                lane = n % self.NLANES
                sem = (eng, f"d{lane}")
                prev = self.last_lane.get(sem)
                if prev is not None:
                    deps.add(prev)
                self.last_lane[sem] = i
            else:
                sem = (eng, "c")
        self.ops.append(dict(eng=eng, emit=emit, deps=deps, inc=inc, sem=sem,
                             tag=tag, need=False, target=None))
        for t in reads:
            self.readers.setdefault(t, []).append(i)
        for t in writes:
            self.lastw[t] = i
            self.readers[t] = []
        return i

    def dma(self, queue, out_ap, in_ap, reads=(), writes=(), tag="",
            transpose=False):
        def emit(e, _o=out_ap, _i=in_ap, _t=transpose):
            if _t:
                return e.dma_start(out=_o, in_=_i, transpose=True)
            return e.dma_start(out=_o, in_=_i)
        return self.add(queue, emit, reads=reads, writes=writes, inc=16, tag=tag)

    def cc(self, kind, op, ins, outs, reads=(), writes=(), tag="cc"):
        k = ("cc", self.n_cc)
        self.n_cc += 1

        def emit(e, _k=kind, _op=op, _i=ins, _o=outs):
            return e.collective_compute(_k, _op, replica_groups=GROUPS,
                                        ins=_i, outs=_o)
        return self.add("gpsimd", emit, reads=reads, writes=writes, inc=1,
                        sem=k, tag=tag)

    def barrier(self, name, tokens, eng="vector"):
        return self.add(eng, lambda e: e.engine_nop(),
                        writes=(name,) + tuple(tokens), tag=f"bar_{name}")

    def emit_all(self, nc, ctx):
        for o in self.ops:
            for d in o["deps"]:
                self.ops[d]["need"] = True
        for o in self.ops:
            if o["inc"] == 16:
                o["need"] = True
        cum = {}
        for o in self.ops:
            if o["need"]:
                k = o["sem"]
                cum[k] = cum.get(k, 0) + o["inc"]
                o["target"] = cum[k]
        semkeys = sorted(set(o["sem"] for o in self.ops if o["need"]),
                         key=lambda k: str(k))
        sems = {k: ctx.enter_context(nc.semaphore(f"s_{k[0]}_{k[1]}"))
                for k in semkeys}
        per_eng = {e: [] for e in ENGINES}
        for i, o in enumerate(self.ops):
            per_eng[o["eng"]].append(i)
        known = {e: {} for e in ENGINES}

        def mk_stream(ename):
            def stream(eng):
                for i in per_eng[ename]:
                    o = self.ops[i]
                    waits = {}
                    for d in o["deps"]:
                        od = self.ops[d]
                        v = od["target"]
                        assert v is not None, (i, d, od["tag"])
                        k = od["sem"]
                        if waits.get(k, 0) < v:
                            waits[k] = v
                    for k in sorted(waits, key=lambda kk: str(kk)):
                        v = waits[k]
                        if known[ename].get(k, 0) >= v:
                            continue
                        eng.wait_ge(sems[k], v)
                        known[ename][k] = v
                    inst = o["emit"](eng)
                    if o["need"]:
                        assert inst is not None, o["tag"]
                        inst.then_inc(sems[o["sem"]], o["inc"])
            return stream

        with nc.Block() as block:
            block.tensor(mk_stream("tensor"))
            block.vector(mk_stream("vector"))
            block.scalar(mk_stream("scalar"))
            block.gpsimd(mk_stream("gpsimd"))
            block.sync(mk_stream("sync"))


def build_nc(cfg):
    L, H, DIN, NST, KCV, R = (cfg[k] for k in ("L", "H", "DIN", "NST", "KC", "R"))
    DH = DIN // 2
    HH = H // 2
    L2 = L // 2
    Lc = L2
    LQ = L // 4
    KH = H // 128
    DC = DH // 128
    KR = R // 128
    KD = DIN // 128
    NCC = min(512, HH)
    CC = HH // NCC
    NTC = L2 // 128
    SUB = min(512, Lc)
    NSB = Lc // SUB
    E = R + 2 * NST
    NH = NST // 2
    assert R % 128 == 0 and DH % 128 == 0 and H % 256 == 0 and L % 256 == 0
    assert LQ <= 512 and KD % 8 == 0 and CC <= 4 and NSB <= 2

    nc = bass.Bass("TRN2", target_bir_lowering=False)

    def din(name, shape):
        return nc.dram_tensor(name, list(shape), F32, kind="ExternalInput")

    x_ln = din("x_ln", (L2, H))
    x_res = din("x_res", (L, HH))
    w_in = din("w_in", (2 * DC, 128, KH * 128))
    conv_w = din("conv_w", (128, DC * KCV))
    conv_b = din("conv_b", (128, DC))
    dt_bias = din("dt_bias", (128, DC))
    a_log = din("a_log", (128, DC * NST))
    d_par = din("d_par", (128, DC))
    x_projT = din("x_projT", (DH, E))
    dt_projT = din("dt_projT", (R, DH))
    dt_ln = din("dt_ln", (128, KR))
    b_ln = din("b_ln", (NST, 1))
    c_ln = din("c_ln", (NST, 1))
    nmask = din("nmask", (NST, NST * 128))
    ident = din("ident", (128, 128))
    woT = din("woT", (DIN, HH))
    wfT = din("wfT", (H, HH))
    ffn_bias = din("ffn_bias", (1, HH))
    out_t = nc.dram_tensor("out", [L, HH], F32, kind="ExternalOutput")

    CH = 2 if CC >= 2 else 1
    C2 = CC // CH
    KG = KD // 8
    xn_mine = nc.dram_tensor("xn_mine", [L2, H], BF16)
    xn_all = nc.dram_tensor("xn_all", [L, H], BF16)
    xnT_sp = nc.dram_tensor("xnT_sp", [2 * NTC, 128, KH * 128], BF16)
    xc_sp = nc.dram_tensor("xc_sp", [DC, 128, L], BF16)
    sz_sp = nc.dram_tensor("sz_sp", [DC, 128, L], BF16)
    ssmp_b = nc.dram_tensor("ssmp_b", [E, L], F32)
    ssmp_r = nc.dram_tensor("ssmp_r", [E, L], F32)
    dtpT_sp = nc.dram_tensor("dtpT_sp", [KR, 128, DH], BF16)
    alpha_sp = nc.dram_tensor("alpha_sp", [1, HH], F32)
    wsT_sp = nc.dram_tensor("wsT_sp", [CH, KH, 128, C2 * NCC], BF16)
    woT_sp = nc.dram_tensor("woT_sp", [CC, KG, 128, 8 * NCC], BF16)
    y_lt = [nc.dram_tensor(f"y_lt{t}", [DC, 128, Lc], BF16) for t in range(2)]
    yg_lt = [nc.dram_tensor(f"yg_lt{t}", [2 * DC, 128, Lc], BF16) for t in range(2)]
    ffn_o = nc.dram_tensor("ffn_o", [L, HH], F32)

    base0 = nc.sbuf_base
    top = nc.sbuf_top

    def rup(x):
        return (x + 31) // 32 * 32

    def at(name, shape, dtype, offset):
        return nc.alloc_sbuf_tensor_at(name, list(shape), dtype, offset=offset)

    def seq(base, items):
        off = rup(base)
        hs = []
        for nm, shp, dt in items:
            hs.append(at(nm, shp, dt, off))
            off += rup(int(np.prod(shp[1:])) * mybir.dt.size(dt))
        assert off <= top, f"SBUF overflow after {items[-1][0]}: {off} > {top}"
        return hs, off

    (perm_h, perm_end) = seq(base0, [
        ("cw_sb", [128, DC * KCV], F32),
        ("cb_sb", [128, DC], F32),
        ("dtb_sb", [128, DC], F32),
        ("dtln_sb", [128, KR], F32),
        ("d_sb", [128, DC], F32),
        ("ones1f", [1, 128], F32),
        ("ones128", [128, 1], F32),
        ("ones16", [16, 1], F32),
        ("ones16T", [1, 16], F32),
        ("bln_sb", [16, 1], F32),
        ("cln_sb", [16, 1], F32),
        ("nmask_sb", [16, NST * 128], BF16),
        ("ident_bf", [128, 128], BF16),
    ])
    (cw_sb, cb_sb, dtb_sb, dtln_sb, d_sb, ones1f, ones128, ones16, ones16T,
     bln_sb, cln_sb, nmask_sb, ident_bf) = perm_h

    def gsz(items):
        return sum(rup(int(np.prod(shp[1:])) * mybir.dt.size(dt))
                   for _, shp, dt in items)

    P0A_SZ = (3 * rup(H * 4) + rup(64 * 4) + 2 * rup(H * 2) + 2 * rup(HH * 4)
              + 2 * rup(HH * 2) + rup(HH * 4) + rup(DH * 4))
    P4A_SZ = (3 * rup(NST * Lc * 2) + rup(2 * Lc * 2) + 2 * rup(DC * NST * 4)
              + rup(KR * L * 2) + rup(KR * DH * 2))
    A0 = rup(perm_end)
    SZ_A = max(KH * L * 2, KD * Lc * 2, P0A_SZ, P4A_SZ)
    B0 = A0 + SZ_A
    assert B0 < top, f"A slot too large: {B0} > {top}"

    (p0a_h, _) = seq(A0, [
        ("lnb0", [128, H], F32), ("lnb1", [128, H], F32),
        ("sqd", [128, H], F32),
        ("lns", [128, 64], F32),
        ("xnb0", [128, H], BF16), ("xnb1", [128, H], BF16),
        ("wf0", [128, HH], F32), ("wf1", [128, HH], F32),
        ("sgn0", [128, HH], BF16), ("sgn1", [128, HH], BF16),
        ("alpha_f", [1, HH], F32),
        ("wdt0", [128, DH], F32),
    ])
    (lnb0, lnb1, sqd, lns, xnb0, xnb1, wf0, wf1, sgn0, sgn1, alpha_f, wdt0) = p0a_h
    ((xnT,), _) = seq(A0, [("xnT", [128, KH * L], BF16)])
    (p4a_h, _) = seq(A0, [
        ("Brep", [128, NST * Lc], BF16), ("Crep", [128, NST * Lc], BF16),
        ("dAh", [128, NST * Lc], BF16),
        ("dBr", [128, 2 * Lc], BF16),
        ("A_sb", [128, DC * NST], F32),
        ("hstate", [128, DC * NST], F32),
        ("dtrT", [128, KR * L], BF16),
        ("dtpT_sb", [128, KR * DH], BF16),
    ])
    (Brep, Crep, dAh, dBr, A_sb, hstate, dtrT, dtpT_sb) = p4a_h
    ((ygt,), _) = seq(A0, [("ygt", [128, KD * Lc], BF16)])

    (p1b_h, _) = seq(B0, [
        ("wcb0", [128, KH * 128], BF16), ("wcb1", [128, KH * 128], BF16),
        ("wcbf", [128, KH * 128], F32),
        ("stg0", [128, 4 + L], F32), ("stg1", [128, 4 + L], F32),
        ("cacc", [128, L], F32),
        ("xcb0", [128, L], BF16), ("xcb1", [128, L], BF16),
        ("szb0", [128, L], BF16), ("szb1", [128, L], BF16),
    ])
    (wcb0, wcb1, wcbf, stg0, stg1, cacc, xcb0, xcb1, szb0, szb1) = p1b_h
    (p2c_h, _) = seq(B0, [
        ("xpT_sb", [128, DC * E], BF16),
        ("xsb0", [128, L2], BF16), ("xsb1", [128, L2], BF16),
    ])
    (xpT_sb, xsb0, xsb1) = p2c_h
    (p3a_h, _) = seq(A0, [
        ("sq3", [128, L], F32),
        ("rsd_r", [128, L], F32),
        ("arow3", [1, max(L, HH)], F32),
        ("brow3", [1, max(L, HH)], F32),
        ("ssm_dt", [128, KR * L], F32),
        ("ssm_B", [16, L], F32), ("ssm_C", [16, L], F32),
    ])
    (sq3, rsd_r, arow3, brow3, ssm_dt, ssm_B, ssm_C) = p3a_h
    (p4b_h, p4b_end) = seq(B0, [
        ("dtf", [128, Lc], F32), ("esp", [128, Lc], F32),
        ("dtxb", [128, Lc], BF16),
        ("ybf", [128, Lc], BF16),
        ("xcd0", [128, Lc], BF16), ("xcd1", [128, Lc], BF16),
        ("szd0", [128, Lc], BF16), ("szd1", [128, Lc], BF16),
    ])
    (dtf, esp, dtxb, ybf, xcd0, xcd1, szd0, szd1) = p4b_h
    (f1_h, fend) = seq(p4b_end, [
        ("fxn0", [128, KH * 128], BF16), ("fxn1", [128, KH * 128], BF16),
        ("fws0", [128, C2 * NCC], BF16), ("fws1", [128, C2 * NCC], BF16),
        ("fst0", [128, NCC], F32),
        ("alph_r", [128, HH], F32), ("bias_r", [128, HH], F32),
        ("Bn_bf", [16, L], BF16), ("Cn_bf", [16, L], BF16),
        ("rowF", [1, HH], F32),
    ])
    (fxn0, fxn1, fws0, fws1, fst0, alph_r, bias_r, Bn_bf, Cn_bf, rowF) = f1_h
    fxns = (fxn0, fxn1)
    dtf2 = at("dtf2", [128, Lc], F32, fend - rup(HH * 4))
    (p5b_h, _) = seq(B0, [
        ("ost0", [128, HH], F32), ("ost1", [128, HH], F32),
        ("ost2", [128, HH], F32), ("ost3", [128, HH], F32),
        ("xfb0", [128, NCC], F32), ("xfb1", [128, NCC], F32),
        ("ffb0", [128, NCC], F32), ("ffb1", [128, NCC], F32),
        ("owb0", [128, 8 * NCC], BF16), ("owb1", [128, 8 * NCC], BF16),
    ])
    (ost0, ost1, ost2, ost3, xfb0, xfb1, ffb0, ffb1, owb0, owb1) = p5b_h
    osts = (ost0, ost1, ost2, ost3)

    psA = nc.alloc_psum_tensor("psA", [128, 4 * 512], F32)
    psB = nc.alloc_psum_tensor("psB", [128, 4 * 512], F32)

    def bank(i, rows=128, width=512):
        t = psA if i < 4 else psB
        return t[0:rows, (i % 4) * 512:(i % 4) * 512 + width]

    PB = tuple(f"pb{i}" for i in range(8))
    S = Sched()

    NT0 = L2 // 128
    for t in range(NT0):
        lb = (lnb0, lnb1)[t % 2]
        xb = (xnb0, xnb1)[t % 2]
        tok = f"lnb{t % 2}"
        xtok = f"xnb{t % 2}"
        stok = f"lns{t % 2}"
        c0 = 8 * (t % 2)
        sm, sq, mc, rs, nb = (lns[:, c0 + k:c0 + k + 1] for k in range(5))
        S.dma("sync", lb[:, 0:H], x_ln.ap()[t * 128:(t + 1) * 128, :],
              reads=(tok,), writes=(tok,), tag=f"ln_ld{t}")
        S.add("vector", lambda e, lb=lb, sm=sm: e.reduce_sum(sm, lb[:, 0:H],
                                                             axis=AX.X),
              reads=(tok,), writes=(stok,), tag=f"ln_sum{t}")
        S.add("scalar", lambda e, lb=lb, sq=sq: e.activation(
            sqd[:, 0:H], lb[:, 0:H], AF.Square, accum_out=sq),
            reads=(tok, "sqdump"), writes=(stok, "sqdump"), tag=f"ln_sq{t}")
        S.add("vector", lambda e, sm=sm, mc=mc: e.tensor_scalar(
            mc, sm, 1.0 / H, None, ALU.mult),
            reads=(stok,), writes=(stok,), tag=f"ln_st1{t}")
        S.add("vector", lambda e, sq=sq, rs=rs: e.tensor_scalar(
            rs, sq, 1.0 / H, EPS_LN, ALU.mult, ALU.add),
            reads=(stok,), writes=(stok,), tag=f"ln_st2{t}")
        S.add("vector", lambda e, sm=sm, mc=mc: e.tensor_tensor(
            sm, mc, mc, ALU.mult),
            reads=(stok,), writes=(stok,), tag=f"ln_st3{t}")
        S.add("vector", lambda e, rs=rs, sm=sm: e.tensor_tensor(
            rs, rs, sm, ALU.subtract),
            reads=(stok,), writes=(stok,), tag=f"ln_st4{t}")
        S.add("scalar", lambda e, rs=rs: e.activation(rs, rs, AF.Sqrt),
              reads=(stok,), writes=(stok,), tag=f"ln_rt{t}")
        S.add("vector", lambda e, rs=rs: e.reciprocal(rs, rs),
              reads=(stok,), writes=(stok,), tag=f"ln_fi1{t}")
        S.add("vector", lambda e, nb=nb, mc=mc, rs=rs: e.tensor_tensor(
            nb, mc, rs, ALU.mult),
            reads=(stok,), writes=(stok,), tag=f"ln_fi2{t}")
        S.add("vector", lambda e, nb=nb: e.tensor_scalar(
            nb, nb, -1.0, None, ALU.mult),
            reads=(stok,), writes=(stok,), tag=f"ln_fi3{t}")
        S.add("vector", lambda e, lb=lb, xb=xb, rs=rs, nb=nb: e.tensor_scalar(
            xb[:, 0:H], lb[:, 0:H], rs, nb, ALU.mult, ALU.add),
            reads=(tok, stok, xtok), writes=(xtok,), tag=f"ln_no{t}")
        S.dma("sync", xn_mine.ap()[t * 128:(t + 1) * 128, :], xb[:, 0:H],
              reads=(xtok,), writes=("xn_mine",), tag=f"ln_wr{t}")
    S.cc("AllGather", ALU.bypass, [xn_mine.ap().opt()], [xn_all.ap().opt()],
         reads=("xn_mine",), writes=("xn_all",), tag="ag_xn")

    for nm_, dst, src in (("cw", cw_sb, conv_w), ("cb", cb_sb, conv_b),
                          ("dtb", dtb_sb, dt_bias), ("dtln", dtln_sb, dt_ln),
                          ("dpar", d_sb, d_par), ("bln", bln_sb, b_ln),
                          ("cln", cln_sb, c_ln)):
        S.dma("sync", dst[:, :], src.ap(), writes=(nm_,), tag=f"ld_{nm_}")
    S.dma("gpsimd", nmask_sb[:, :], nmask.ap(), writes=("nmask",), tag="ld_nmask")
    S.dma("gpsimd", ident_bf[:, :], ident.ap(), writes=("ident",), tag="ld_ident")
    for nm_, h in (("ones1f", ones1f), ("ones128", ones128),
                   ("ones16", ones16), ("o16T", ones16T)):
        S.add("vector", lambda e, h=h: e.memset(h[:, :], 1.0), writes=(nm_,),
              tag=f"ms_{nm_}")

    for kr in range(KR):
        S.dma("sync", wdt0[:, 0:DH], dt_projT.ap()[kr * 128:(kr + 1) * 128, :],
              reads=("wdt",), writes=("wdt",), tag=f"dtp_ld{kr}")
        S.add("vector", lambda e, kr=kr: e.tensor_scalar(
            wdt0[:, 0:DH], wdt0[:, 0:DH], dtln_sb[:, kr:kr + 1], None, ALU.mult),
            reads=("wdt", "dtln"), writes=("wdt",), tag=f"dtp_f{kr}")
        S.add("vector", lambda e: e.tensor_copy(xnb0[:, 0:DH], wdt0[:, 0:DH]),
              reads=("wdt", "xnb0"), writes=("xnb0",), tag=f"dtp_c{kr}")
        S.dma("sync", dtpT_sp.ap()[kr, :, :], xnb0[:, 0:DH],
              reads=("xnb0",), writes=("dtpT_sp",), tag=f"dtp_wr{kr}")

    for kc in range(KH):
        wb = (wf0, wf1)[kc % 2]
        sg = (sgn0, sgn1)[kc % 2]
        wtok = f"wf{kc % 2}"
        gtok = f"sgn{kc % 2}"
        S.dma("scalar", wb[:, :], wfT.ap()[kc * 128:(kc + 1) * 128, :],
              reads=(wtok,), writes=(wtok,), tag=f"fp_ld{kc}")
        S.add("scalar", lambda e, wb=wb, sg=sg: e.activation(sg[:, :], wb[:, :],
                                                             AF.Sign),
              reads=(wtok, gtok), writes=(gtok,), tag=f"fp_sg{kc}")
        S.add("scalar", lambda e, wb=wb: e.activation(wb[:, :], wb[:, :], AF.Abs),
              reads=(wtok,), writes=(wtok,), tag=f"fp_ab{kc}")

        def mm_alpha(e, wb=wb, kc=kc):
            last = None
            for c in range(CC):
                last = e.matmul(bank(c, rows=1, width=NCC), ones128[:, :],
                                wb[:, c * NCC:(c + 1) * NCC],
                                start=(kc == 0), stop=(kc == KH - 1))
            return last
        S.add("tensor", mm_alpha, reads=(wtok, "ones128"), writes=PB[:CC],
              tag=f"fp_mm{kc}")
        for ch in range(CH):
            S.dma("sync", wsT_sp.ap()[ch, kc, :, :],
                  sg[:, ch * C2 * NCC:(ch + 1) * C2 * NCC],
                  reads=(gtok,), writes=("wsT_sp",), tag=f"fp_wr{kc}_{ch}")

    def ev_alpha(e):
        last = None
        for c in range(CC):
            last = e.tensor_copy(alpha_f[0:1, c * NCC:(c + 1) * NCC],
                                 bank(c, rows=1, width=NCC))
        return last
    S.add("vector", ev_alpha, reads=PB[:CC] + ("alphaF",),
          writes=("alphaF",) + PB[:CC], tag="ev_alpha")
    S.add("vector", lambda e: e.tensor_scalar(
        alpha_f[0:1, :], alpha_f[0:1, :], 1.0 / H, None, ALU.mult),
        reads=("alphaF",), writes=("alphaF",), tag="ev_alpha2")
    S.dma("sync", alpha_sp.ap(), alpha_f[0:1, :], reads=("alphaF",),
          writes=("alpha_sp",), tag="alpha_wr")

    for kd in range(KD):
        kg, ki = kd // 8, kd % 8
        in_ap = woT.ap()[kd * 128:(kd + 1) * 128, :].rearrange(
            "p (cc s) -> p cc s", cc=CC)
        out_ap = woT_sp.ap()[:, kg, :, ki * NCC:(ki + 1) * NCC].rearrange(
            "cc p s -> p cc s")
        S.dma("gpsimd", out_ap, in_ap, writes=("woT_sp",), tag=f"wo_c{kd}")

    S.barrier("p0a_dead", ["lnb0", "lnb1", "xnb0", "xnb1", "sqdump",
                           "lns0", "lns1", "wdt"])
    for kc in range(KH):
        S.dma("sync", xnT[:, kc * L:(kc + 1) * L],
              xn_all.ap()[:, kc * 128:(kc + 1) * 128],
              reads=("xn_all", "p0a_dead"), writes=("xnT",), tag=f"xnT{kc}",
              transpose=True)
    for tc in range(2 * NTC):
        in_ap = xnT.ap().rearrange("p (kc l) -> p kc l", kc=KH)[
            :, :, tc * 128:(tc + 1) * 128]
        S.dma("gpsimd", xnT_sp.ap()[tc, :, :].rearrange(
            "p (kc s) -> p kc s", kc=KH), in_ap,
            reads=("xnT",), writes=("xnT_sp",), tag=f"xnsp{tc}")

    S.add("vector", lambda e: e.memset(stg0[:, 0:4], 0.0),
          reads=("p0a_dead",), writes=("stgp0",), tag="pad0")
    S.add("vector", lambda e: e.memset(stg1[:, 0:4], 0.0),
          reads=("p0a_dead",), writes=("stgp1",), tag="pad1")

    NCB = 2 * DC

    def w_load(c):
        par = c % 2
        wb = (wcb0, wcb1)[par]
        wtok = f"wcb{par}"
        S.dma("sync", wcbf[:, :], w_in.ap()[c, :, :],
              reads=("wcbf", "p0a_dead"), writes=("wcbf",), tag=f"w_inf{c}")
        S.add("scalar", lambda e, wb=wb: e.activation(wb[:, :], wcbf[:, :],
                                                      AF.Copy),
              reads=("wcbf", wtok), writes=(wtok, "wcbf"), tag=f"w_cast{c}")

    w_load(0)
    for c in range(NCB):
        par = c % 2
        wb = (wcb0, wcb1)[par]
        wtok = f"wcb{par}"
        if c + 1 < NCB:
            w_load(c + 1)
        pbt = tuple(f"pb{par * 4 + q}" for q in range(4))

        def mm_ip(e, wb=wb, par=par):
            last = None
            for kc in range(KH):
                for lq in range(4):
                    last = e.matmul(
                        bank(par * 4 + lq, width=LQ),
                        wb[:, kc * 128:(kc + 1) * 128],
                        xnT[:, kc * L + lq * LQ:kc * L + (lq + 1) * LQ],
                        start=(kc == 0), stop=(kc == KH - 1))
            return last
        S.add("tensor", mm_ip, reads=(wtok, "xnT"), writes=pbt, tag=f"ip_mm{c}")
        if c < DC:
            stg = (stg0, stg1)[par]
            xcb = (xcb0, xcb1)[par]
            sgk = f"stg{par}"
            xck = f"xcb{par}"

            def ev_x(e, stg=stg, par=par):
                last = None
                for lq in range(4):
                    last = e.tensor_copy(
                        stg[:, 4 + lq * LQ:4 + (lq + 1) * LQ],
                        bank(par * 4 + lq, width=LQ))
                return last
            S.add("vector", ev_x, reads=pbt + (f"stgp{par}", sgk),
                  writes=(sgk,) + pbt, tag=f"ip_evx{c}")
            S.add("vector", lambda e, stg=stg, dc=c: e.tensor_scalar(
                cacc[:, 0:L], stg[:, 1:1 + L],
                cw_sb[:, dc * KCV:dc * KCV + 1],
                cb_sb[:, dc:dc + 1], ALU.mult, ALU.add),
                reads=(sgk, "cw", "cb", "cacc"), writes=("cacc",),
                tag=f"conv{c}_0")
            for k in range(1, KCV):
                S.add("vector", lambda e, stg=stg, dc=c, k=k:
                      e.scalar_tensor_tensor(
                          cacc[:, 0:L], stg[:, 1 + k:1 + k + L],
                          cw_sb[:, dc * KCV + k:dc * KCV + k + 1],
                          cacc[:, 0:L], ALU.mult, ALU.add),
                      reads=(sgk, "cw", "cacc"),
                      writes=("cacc",) + ((sgk,) if k == KCV - 1 else ()),
                      tag=f"conv{c}_{k}")
            S.add("scalar", lambda e, xcb=xcb: e.activation(
                xcb[:, 0:L], cacc[:, 0:L], AF.Silu),
                reads=("cacc", xck), writes=(xck,), tag=f"si_xc{c}")
            S.dma("sync", xc_sp.ap()[c, :, :], xcb[:, 0:L],
                  reads=(xck,), writes=("xc_sp",), tag=f"xc_wr{c}")
        else:
            dc = c - DC
            szb = (szb0, szb1)[par]
            szk = f"szb{par}"

            def ev_z(e, szb=szb, par=par):
                last = None
                for lq in range(4):
                    last = e.activation(
                        szb[:, lq * LQ:(lq + 1) * LQ],
                        bank(par * 4 + lq, width=LQ), AF.Silu)
                return last
            S.add("scalar", ev_z, reads=pbt + (szk,), writes=(szk,) + pbt,
                  tag=f"ip_evz{c}")
            S.dma("sync", sz_sp.ap()[dc, :, :], szb[:, 0:L],
                  reads=(szk,), writes=("sz_sp",), tag=f"sz_wr{dc}")

    S.barrier("p1w_dead", ["wcb0", "wcb1", "wcbf", "stg0", "stg1", "cacc",
                           "xcb0", "xcb1", "szb0", "szb1", "stgp0", "stgp1"])
    S.barrier("xnT_dead", ["xnT", "xnT_sp"])
    for dc in range(DC):
        S.dma("gpsimd", xpT_sb[:, dc * E:(dc + 1) * E],
              x_projT.ap()[dc * 128:(dc + 1) * 128, :],
              reads=("p1w_dead", "xpT"), writes=("xpT",), tag=f"xp_ld{dc}")
    e_chunks = [(kr * 128, 128, "r", kr) for kr in range(KR)] + \
               [(R, NST, "B", 0), (R + NST, NST, "C", 0)]
    n_e = len(e_chunks)
    assert 2 * n_e <= 8 and L2 // LQ == 2
    for ltp in range(2):
        for dc in range(DC):
            xb = (xsb0, xsb1)[dc % 2]
            xk = f"xsb{dc % 2}"
            S.dma("sync", xb[:, 0:L2], xc_sp.ap()[dc, :, ltp * L2:(ltp + 1) * L2],
                  reads=("xc_sp", xk, "p1w_dead"), writes=(xk,),
                  tag=f"sp_ld{ltp}_{dc}")

            def mm_sp(e, xb=xb, dc=dc):
                last = None
                for ei, (eoff, rows, _, _) in enumerate(e_chunks):
                    for l4 in range(2):
                        last = e.matmul(
                            bank(ei * 2 + l4, rows=rows, width=LQ),
                            xpT_sb[:, dc * E + eoff:dc * E + eoff + rows],
                            xb[:, l4 * LQ:(l4 + 1) * LQ],
                            start=(dc == 0), stop=(dc == DC - 1))
                return last
            S.add("tensor", mm_sp, reads=(xk, "xpT"),
                  writes=tuple(f"pb{i}" for i in range(2 * n_e)) + (xk,),
                  tag=f"sp_mm{ltp}_{dc}")

        def ev_sp(e, ltp=ltp):
            last = None
            for ei, (eoff, rows, kind, kr) in enumerate(e_chunks):
                for l4 in range(2):
                    o = ltp * L2 + l4 * LQ
                    if kind == "r":
                        dst = ssm_dt[:, kr * L + o:kr * L + o + LQ]
                    elif kind == "B":
                        dst = ssm_B[:, o:o + LQ]
                    else:
                        dst = ssm_C[:, o:o + LQ]
                    last = e.tensor_copy(dst, bank(ei * 2 + l4, rows=rows,
                                                   width=LQ))
            return last
        S.add("vector", ev_sp,
              reads=tuple(f"pb{i}" for i in range(2 * n_e)) + ("ssm_sb",
                                                              "xnT_dead"),
              writes=("ssm_sb",) + tuple(f"pb{i}" for i in range(2 * n_e)),
              tag=f"sp_ev{ltp}")
    for kr in range(KR):
        S.dma("sync", ssmp_b.ap()[kr * 128:(kr + 1) * 128, :],
              ssm_dt[:, kr * L:(kr + 1) * L], reads=("ssm_sb",),
              writes=("ssmp_b",), tag=f"sp_wr{kr}")
    S.dma("sync", ssmp_b.ap()[R:R + NST, :], ssm_B[:, 0:L],
          reads=("ssm_sb",), writes=("ssmp_b",), tag="sp_wrB")
    S.dma("sync", ssmp_b.ap()[R + NST:R + 2 * NST, :], ssm_C[:, 0:L],
          reads=("ssm_sb",), writes=("ssmp_b",), tag="sp_wrC")
    S.cc("AllReduce", ALU.add, [ssmp_b.ap().opt()], [ssmp_r.ap().opt()],
         reads=("ssmp_b",), writes=("ssmp_r",), tag="ar_ssm")
    S.barrier("p2x_dead", ["xpT", "xsb0", "xsb1"])

    S.dma("sync", rowF[0:1, :], alpha_sp.ap(), reads=("alpha_sp", "p1w_dead"),
          writes=("rowF",), tag="ld_rowA")

    def mm_ar(e):
        last = None
        for c in range(CC):
            last = e.matmul(bank(c, width=NCC), ones1f[:, :],
                            rowF[0:1, c * NCC:(c + 1) * NCC],
                            start=True, stop=True)
        return last
    S.add("tensor", mm_ar, reads=("rowF", "ones1f"), writes=PB[:CC],
          tag="f_mmar")

    def ev_ar(e):
        last = None
        for c in range(CC):
            last = e.tensor_copy(alph_r[:, c * NCC:(c + 1) * NCC],
                                 bank(c, width=NCC))
        return last
    S.add("vector", ev_ar, reads=PB[:CC] + ("alphR", "p1w_dead"),
          writes=("alphR",) + PB[:CC], tag="f_evar")
    S.dma("sync", rowF[0:1, :], ffn_bias.ap(), reads=("rowF",),
          writes=("rowF",), tag="ld_rowB")

    def mm_br(e):
        last = None
        for c in range(CC):
            last = e.matmul(bank(c, width=NCC), ones1f[:, :],
                            rowF[0:1, c * NCC:(c + 1) * NCC],
                            start=True, stop=True)
        return last
    S.add("tensor", mm_br, reads=("rowF", "ones1f"), writes=PB[:CC],
          tag="f_mmbr")

    def ev_br(e):
        last = None
        for c in range(CC):
            last = e.tensor_copy(bias_r[:, c * NCC:(c + 1) * NCC],
                                 bank(c, width=NCC))
        return last
    S.add("vector", ev_br, reads=PB[:CC] + ("biasR", "p1w_dead"),
          writes=("biasR",) + PB[:CC], tag="f_evbr")

    TPG_F = min(2, NTC)
    NTG_F = NTC // TPG_F

    def ffn_unit(tch, ch, tg):
        for t2 in range(TPG_F):
            tc = tch * NTC + tg * TPG_F + t2
            fx = fxns[t2]
            S.dma("scalar", fx[:, :], xnT_sp.ap()[tc, :, :],
                  reads=("xnT_sp", f"fx{t2}", "p2x_dead"),
                  writes=(f"fx{t2}",), tag=f"f_x{tch}_{ch}_{tg}_{t2}")
        fxtoks = tuple(f"fx{t2}" for t2 in range(TPG_F))
        yield
        for kc in range(KH):
            fw = (fws0, fws1)[kc % 2]
            fwk = f"fws{kc % 2}"
            S.dma("scalar", fw[:, :], wsT_sp.ap()[ch % CH, kc, :, :],
                  reads=("wsT_sp", fwk, "p1w_dead"), writes=(fwk,),
                  tag=f"f_w{tch}_{ch}_{tg}_{kc}")

            def mm_f(e, fw=fw, kc=kc):
                last = None
                for t2 in range(TPG_F):
                    fx = fxns[t2]
                    for c2 in range(C2):
                        last = e.matmul(
                            bank(4 + t2 * C2 + c2, width=NCC),
                            fx[:, kc * 128:(kc + 1) * 128],
                            fw[:, c2 * NCC:(c2 + 1) * NCC],
                            start=(kc == 0), stop=(kc == KH - 1))
                return last
            S.add("tensor", mm_f, reads=(fwk,) + fxtoks,
                  writes=tuple(f"pb{4 + t2 * C2 + c2}"
                               for t2 in range(TPG_F)
                               for c2 in range(C2)) + (fwk,),
                  tag=f"f_mm{tch}_{ch}_{tg}_{kc}")
            yield
        for t2 in range(TPG_F):
            tc = tch * NTC + tg * TPG_F + t2
            for c2 in range(C2):
                c = (ch % CH) * C2 + c2
                S.add("vector", lambda e, t2=t2, c2=c2, c=c:
                      e.tensor_tensor(fst0[:, 0:NCC],
                                      bank(4 + t2 * C2 + c2, width=NCC),
                                      alph_r[:, c * NCC:(c + 1) * NCC],
                                      ALU.mult),
                      reads=(f"pb{4 + t2 * C2 + c2}", "alphR", "fst0"),
                      writes=("fst0", f"pb{4 + t2 * C2 + c2}"),
                      tag=f"f_ev{tch}_{ch}_{tg}_{t2}_{c2}")
                S.add("vector", lambda e, c=c: e.tensor_tensor(
                    fst0[:, 0:NCC], fst0[:, 0:NCC],
                    bias_r[:, c * NCC:(c + 1) * NCC], ALU.add),
                    reads=("biasR", "fst0"), writes=("fst0",),
                    tag=f"f_ev2{tch}_{ch}_{tg}_{t2}_{c2}")
                S.dma("scalar",
                      ffn_o.ap()[tc * 128:(tc + 1) * 128,
                                 c * NCC:(c + 1) * NCC],
                      fst0[:, 0:NCC], reads=("fst0",), writes=("ffn_o",),
                      tag=f"f_st{tch}_{ch}_{tg}_{t2}_{c2}")

    ffn_units = [(tch, ch, tg) for tch in range(2) for ch in range(CH)
                 for tg in range(NTG_F)]
    n_quanta = len(ffn_units) * (2 + KH)
    total_dc = 2 * DC

    def ffn_gen():
        for u in ffn_units:
            yield from ffn_unit(*u)
            yield
    ffn_it = [ffn_gen(), 0]

    def pop_ffn(frac):
        want = int(frac * n_quanta + 1e-9)
        while ffn_it[1] < want:
            try:
                next(ffn_it[0])
            except StopIteration:
                pass
            ffn_it[1] += 1

    pop_ffn(6.0 / n_quanta)

    for kr in range(KR):
        S.dma("sync", ssm_dt[:, kr * L:(kr + 1) * L],
              ssmp_r.ap()[kr * 128:(kr + 1) * 128, :],
              reads=("ssmp_r", "ssm_sb"), writes=("ssm_sb",), tag=f"n_ld{kr}")
    S.dma("sync", ssm_B[:, 0:L], ssmp_r.ap()[R:R + NST, :],
          reads=("ssmp_r",), writes=("ssm_sbB",), tag="n_ldB")
    S.dma("sync", ssm_C[:, 0:L], ssmp_r.ap()[R + NST:R + 2 * NST, :],
          reads=("ssmp_r",), writes=("ssm_sbC",), tag="n_ldC")
    S.dma("sync", A_sb[:, :], a_log.ap(), reads=("xnT_dead", "A"), writes=("A",),
          tag="ld_alog")
    S.add("scalar", lambda e: e.activation(A_sb[:, :], A_sb[:, :], AF.Exp),
          reads=("A",), writes=("A",), tag="a_exp")
    S.add("vector", lambda e: e.tensor_scalar(A_sb[:, :], A_sb[:, :], -1.0,
                                              None, ALU.mult),
          reads=("A",), writes=("A",), tag="a_neg")
    S.add("vector", lambda e: e.memset(hstate[:, :], 0.0),
          reads=("xnT_dead",), writes=("hst",), tag="h_zero")
    for kr in range(KR):
        S.dma("sync", dtpT_sb[:, kr * DH:(kr + 1) * DH], dtpT_sp.ap()[kr, :, :],
              reads=("dtpT_sp", "xnT_dead", "dtpT"), writes=("dtpT",),
              tag=f"dtp_rl{kr}")
    WL = min(512, L)
    NL4 = L // WL
    assert NL4 <= 4
    for kr in range(KR):
        S.add("scalar", lambda e, kr=kr: e.activation(
            sq3[:, 0:L], ssm_dt[:, kr * L:(kr + 1) * L], AF.Square),
            reads=("ssm_sb", "sq3", "p2x_dead"), writes=("sq3",), tag=f"n_sq{kr}")

        def mm_rs(e, kr=kr):
            last = None
            for l4 in range(NL4):
                last = e.matmul(bank(l4, rows=1, width=WL), ones128[:, :],
                                sq3[:, l4 * WL:(l4 + 1) * WL],
                                start=(kr == 0), stop=(kr == KR - 1))
            return last
        S.add("tensor", mm_rs, reads=("sq3", "ones128"), writes=PB[:NL4],
              tag=f"n_mm{kr}")

    def ev_rs(e):
        last = None
        for l4 in range(NL4):
            last = e.tensor_copy(arow3[0:1, l4 * WL:(l4 + 1) * WL],
                                 bank(l4, rows=1, width=WL))
        return last
    S.add("vector", ev_rs, reads=PB[:NL4] + ("arow3", "p2x_dead"),
          writes=("arow3",) + PB[:NL4], tag="n_ev")
    S.add("vector", lambda e: e.tensor_scalar(
        arow3[0:1, 0:L], arow3[0:1, 0:L], 1.0 / R, EPS_RMS, ALU.mult, ALU.add),
        reads=("arow3",), writes=("arow3",), tag="n_ev2")
    S.add("scalar", lambda e: e.activation(arow3[0:1, 0:L], arow3[0:1, 0:L],
                                           AF.Sqrt),
          reads=("arow3",), writes=("arow3",), tag="n_sqrt")
    S.add("vector", lambda e: e.reciprocal(arow3[0:1, 0:L], arow3[0:1, 0:L]),
          reads=("arow3",), writes=("arow3",), tag="n_rec")

    def mm_rrep(e):
        last = None
        for l4 in range(NL4):
            last = e.matmul(bank(l4, width=WL), ones1f[:, :],
                            arow3[0:1, l4 * WL:(l4 + 1) * WL],
                            start=True, stop=True)
        return last
    S.add("tensor", mm_rrep, reads=("arow3", "ones1f"), writes=PB[:NL4],
          tag="n_mmrep")

    def ev_rrep(e):
        last = None
        for l4 in range(NL4):
            last = e.tensor_copy(rsd_r[:, l4 * WL:(l4 + 1) * WL],
                                 bank(l4, width=WL))
        return last
    S.add("vector", ev_rrep, reads=PB[:NL4] + ("rsd", "p2x_dead"),
          writes=("rsd",) + PB[:NL4], tag="n_evrep")
    for kr in range(KR):
        S.add("vector", lambda e, kr=kr: e.tensor_tensor(
            dtrT[:, kr * L:(kr + 1) * L], ssm_dt[:, kr * L:(kr + 1) * L],
            rsd_r[:, 0:L], ALU.mult),
            reads=("ssm_sb", "rsd", "xnT_dead", "dtrT"), writes=("dtrT",),
            tag=f"n_dtrT{kr}")
    for nm_, src, srctok, lncol, lntok, dst, dsttok in (
            ("B", ssm_B, "ssm_sbB", bln_sb, "bln", Bn_bf, "Bn"),
            ("C", ssm_C, "ssm_sbC", cln_sb, "cln", Cn_bf, "Cn")):
        S.add("scalar", lambda e, src=src: e.activation(
            sq3[0:16, 0:L], src[:, 0:L], AF.Square),
            reads=(srctok, "sq3"), writes=("sq3",), tag=f"n_sq{nm_}")

        def mm_ns(e):
            last = None
            for l4 in range(NL4):
                last = e.matmul(bank(l4, rows=1, width=WL), ones16[:, :],
                                sq3[0:16, l4 * WL:(l4 + 1) * WL],
                                start=True, stop=True)
            return last
        S.add("tensor", mm_ns, reads=("sq3", "ones16"), writes=PB[:NL4],
              tag=f"n_nmm{nm_}")

        def ev_ns(e):
            last = None
            for l4 in range(NL4):
                last = e.tensor_copy(brow3[0:1, l4 * WL:(l4 + 1) * WL],
                                     bank(l4, rows=1, width=WL))
            return last
        S.add("vector", ev_ns, reads=PB[:NL4] + ("brow3",),
              writes=("brow3",) + PB[:NL4], tag=f"n_nev{nm_}")
        S.add("vector", lambda e: e.tensor_scalar(
            brow3[0:1, 0:L], brow3[0:1, 0:L], 1.0 / NST, EPS_RMS,
            ALU.mult, ALU.add),
            reads=("brow3",), writes=("brow3",), tag=f"n_nev2{nm_}")
        S.add("scalar", lambda e: e.activation(brow3[0:1, 0:L],
                                               brow3[0:1, 0:L], AF.Sqrt),
              reads=("brow3",), writes=("brow3",), tag=f"n_nsq{nm_}")
        S.add("vector", lambda e: e.reciprocal(brow3[0:1, 0:L],
                                               brow3[0:1, 0:L]),
              reads=("brow3",), writes=("brow3",), tag=f"n_nrc{nm_}")

        def mm_nrep(e):
            last = None
            for l4 in range(NL4):
                last = e.matmul(bank(l4, rows=16, width=WL),
                                ones16T[0:1, :],
                                brow3[0:1, l4 * WL:(l4 + 1) * WL],
                                start=True, stop=True)
            return last
        S.add("tensor", mm_nrep, reads=("brow3", "o16T"), writes=PB[:NL4],
              tag=f"n_nrep{nm_}")

        def ev_bn1(e, src=src):
            last = None
            for l4 in range(NL4):
                sl = slice(l4 * WL, (l4 + 1) * WL)
                last = e.tensor_tensor(sq3[0:16, sl], src[:, sl],
                                       bank(l4, rows=16, width=WL),
                                       ALU.mult)
            return last
        S.add("vector", ev_bn1,
              reads=(srctok, "sq3") + PB[:NL4],
              writes=("sq3",) + PB[:NL4], tag=f"n_bn1{nm_}")
        S.add("vector", lambda e, lncol=lncol, dst=dst: e.tensor_scalar(
            dst[:, 0:L], sq3[0:16, 0:L], lncol[:, 0:1], None, ALU.mult),
            reads=(lntok, "sq3", "p1w_dead", dsttok),
            writes=(dsttok,), tag=f"n_bn2{nm_}")

    S.barrier("ssm_dead", ["ssm_sb", "ssm_sbB", "ssm_sbC"])
    S.barrier("p3x_dead", ["sq3", "arow3", "brow3", "rsd"])

    sched_dc = [(lt, dc) for lt in range(2) for dc in range(DC)]

    def emit_reps(lt):
        for nm_, srctok, rep, reptok in (("B", "Bn", Brep, "Br"),
                                         ("C", "Cn", Crep, "Cr")):
            srcbf = Bn_bf if nm_ == "B" else Cn_bf
            for n in range(NST):
                pbs = tuple(f"pb{2 + sb}" for sb in range(NSB))

                def mm_rep(e, srcbf=srcbf, n=n, lt=lt):
                    last = None
                    for sb in range(NSB):
                        last = e.matmul(
                            bank(2 + sb, width=SUB),
                            nmask_sb[:, n * 128:(n + 1) * 128],
                            srcbf[:, lt * Lc + sb * SUB:
                                  lt * Lc + (sb + 1) * SUB],
                            start=True, stop=True)
                    return last
                S.add("tensor", mm_rep, reads=(srctok, "nmask", "xnT_dead"),
                      writes=pbs, tag=f"r_mm{nm_}{lt}_{n}")

                def ev_rep(e, rep=rep, n=n):
                    last = None
                    for sb in range(NSB):
                        last = e.tensor_copy(
                            rep[:, n * Lc + sb * SUB:n * Lc + (sb + 1) * SUB],
                            bank(2 + sb, width=SUB))
                    return last
                S.add("vector", ev_rep,
                      reads=pbs + (f"{reptok}{n}", "xnT_dead", "p3x_dead",
                                   "ssm_dead"),
                      writes=(f"{reptok}{n}",) + pbs, tag=f"r_ev{nm_}{lt}_{n}")

    def emit_dt(lt, dc):
        par = dc % 2
        dtfc = (dtf, dtf2)[par]

        def mm_dt(e, dc=dc, lt=lt):
            last = None
            for kr in range(KR):
                for sb in range(NSB):
                    last = e.matmul(
                        bank(sb, width=SUB),
                        dtpT_sb[:, kr * DH + dc * 128:
                                kr * DH + (dc + 1) * 128],
                        dtrT[:, kr * L + lt * Lc + sb * SUB:
                             kr * L + lt * Lc + (sb + 1) * SUB],
                        start=(kr == 0), stop=(kr == KR - 1))
            return last
        pbdt = tuple(f"pb{sb}" for sb in range(NSB))
        S.add("tensor", mm_dt, reads=("dtrT", "dtpT"), writes=pbdt,
              tag=f"s_dtmm{lt}_{dc}")

        def sp1(e, dc=dc):
            last = None
            for sb in range(NSB):
                last = e.activation(esp[:, sb * SUB:(sb + 1) * SUB],
                                    bank(sb, width=SUB), AF.Exp,
                                    bias=dtb_sb[:, dc:dc + 1])
            return last
        S.add("scalar", sp1, reads=pbdt + ("dtb", "esp", "p2x_dead"),
              writes=("esp",) + pbdt, tag=f"s_sp1{lt}_{dc}")
        S.add("scalar", lambda e, dtfc=dtfc: e.activation(
            dtfc[:, 0:Lc], esp[:, 0:Lc], AF.Ln, bias=1.0),
            reads=("esp", f"dtf{par}"),
            writes=(f"dtf{par}",) + (("rowF",) if par == 1 else ()),
            tag=f"s_sp2{lt}_{dc}")

    emit_reps(0)
    emit_dt(*sched_dc[0])
    it = [0]
    for idx, (lt, dc) in enumerate(sched_dc):
        par = dc % 2
        dtfc = (dtf, dtf2)[par]
        xcd = (xcd0, xcd1)[par]
        szd = (szd0, szd1)[par]
        xk = f"xcd{par}"
        zk = f"szd{par}"
        S.dma("sync", xcd[:, 0:Lc], xc_sp.ap()[dc, :, lt * Lc:(lt + 1) * Lc],
              reads=("xc_sp", xk, "p2x_dead"), writes=(xk,),
              tag=f"s_xc{lt}_{dc}")
        S.dma("sync", szd[:, 0:Lc], sz_sp.ap()[dc, :, lt * Lc:(lt + 1) * Lc],
              reads=("sz_sp", zk, "p2x_dead"), writes=(zk,),
              tag=f"s_sz{lt}_{dc}")
        if idx + 1 < len(sched_dc):
            emit_dt(*sched_dc[idx + 1])
        S.add("vector", lambda e, xcd=xcd, dtfc=dtfc: e.tensor_tensor(
            dtxb[:, 0:Lc], dtfc[:, 0:Lc], xcd[:, 0:Lc], ALU.mult),
            reads=(f"dtf{par}", xk, "dtx"), writes=("dtx",),
            tag=f"s_dtx{lt}_{dc}")
        for n in range(NST):
            S.add("scalar", lambda e, n=n, dc=dc, dtfc=dtfc: e.activation(
                dAh[:, n * Lc:(n + 1) * Lc], dtfc[:, 0:Lc], AF.Exp,
                scale=A_sb[:, dc * NST + n:dc * NST + n + 1]),
                reads=(f"dtf{par}", "A", f"dA{n}", "ssm_dead"),
                writes=(f"dA{n}",), tag=f"s_dA{lt}_{dc}_{n}")
            S.add("vector", lambda e, n=n: e.tensor_tensor(
                dBr[:, (n % 2) * Lc:(n % 2) * Lc + Lc], dtxb[:, 0:Lc],
                Brep[:, n * Lc:(n + 1) * Lc], ALU.mult),
                reads=("dtx", f"Br{n}", f"dBr{n % 2}"),
                writes=(f"dBr{n % 2}",), tag=f"s_dB{lt}_{dc}_{n}")
            S.add("vector", lambda e, n=n, dc=dc: e.tensor_tensor_scan(
                dAh[:, n * Lc:(n + 1) * Lc], dAh[:, n * Lc:(n + 1) * Lc],
                dBr[:, (n % 2) * Lc:(n % 2) * Lc + Lc],
                hstate[:, dc * NST + n:dc * NST + n + 1],
                ALU.mult, ALU.add),
                reads=(f"dA{n}", f"dBr{n % 2}", "hst"),
                writes=(f"dA{n}",), tag=f"s_sc{lt}_{dc}_{n}")
            S.add("vector", lambda e, n=n: e.tensor_tensor(
                dBr[:, (n % 2) * Lc:(n % 2) * Lc + Lc],
                dAh[:, n * Lc:(n + 1) * Lc],
                Crep[:, n * Lc:(n + 1) * Lc], ALU.mult),
                reads=(f"dA{n}", f"Cr{n}", f"dBr{n % 2}"),
                writes=(f"dBr{n % 2}",), tag=f"s_hC{lt}_{dc}_{n}")

            def mm_ysum1(e, n=n):
                last = None
                for sb in range(NSB):
                    last = e.matmul(
                        bank(2 + sb, width=SUB), ident_bf[:, :],
                        dBr[:, (n % 2) * Lc + sb * SUB:
                            (n % 2) * Lc + (sb + 1) * SUB],
                        start=(n == 0), stop=(n == NST - 1))
                return last
            S.add("tensor", mm_ysum1,
                  reads=(f"dBr{n % 2}", "ident"),
                  writes=("pb2", "pb3"), tag=f"s_ys{lt}_{dc}_{n}")

        def hupd(e, dc=dc):
            hv = dAh.ap().rearrange("p (n l) -> p n l", n=NST)[:, :, Lc - 1]
            return e.tensor_copy(hstate[:, dc * NST:(dc + 1) * NST], hv)
        S.add("vector", hupd, reads=tuple(f"dA{n}" for n in range(NST))
              + ("hst",), writes=("hst",), tag=f"s_hu{lt}_{dc}")
        S.add("vector", lambda e, xcd=xcd, dc=dc: e.scalar_tensor_tensor(
            ybf[:, 0:Lc], xcd[:, 0:Lc], d_sb[:, dc:dc + 1],
            psA[:, 2 * 512:2 * 512 + Lc], ALU.mult, ALU.add),
            reads=("pb2", "pb3", xk, "dpar", "ybf"),
            writes=("ybf", "pb2", "pb3"),
            tag=f"s_yk{lt}_{dc}")
        S.add("vector", lambda e, szd=szd: e.tensor_tensor(
            ybf[:, 0:Lc], ybf[:, 0:Lc], szd[:, 0:Lc], ALU.mult),
            reads=(zk, "ybf"), writes=("ybf",), tag=f"s_ym{lt}_{dc}")
        S.dma("sync", y_lt[lt].ap()[dc, :, :], ybf[:, 0:Lc],
              reads=("ybf",), writes=(f"y_l{lt}",), tag=f"s_yw{lt}_{dc}")
        it[0] += 1
        pop_ffn(min(1.0, 6.0 / n_quanta + it[0] / total_dc))
        if dc == DC - 1:
            S.cc("AllGather", ALU.bypass, [y_lt[lt].ap().opt()],
                 [yg_lt[lt].ap().opt()], reads=(f"y_l{lt}",),
                 writes=(f"yg{lt}",), tag=f"ag_y{lt}")
            if lt == 0:
                emit_reps(1)
    pop_ffn(1.0)
    try:
        next(ffn_it[0])
        raise AssertionError("ffn generator not fully consumed")
    except StopIteration:
        pass

    S.barrier("p4_dead", ["hst", "ybf", "dtx", "dtf", "esp",
                          "xcd0", "xcd1", "szd0", "szd1",
                          "dtrT", "dtpT", "Bn", "Cn", "A", "dpar", "nmask",
                          "fx0", "fx1", "fws0", "fws1", "fst0", "alphR",
                          "biasR", "rowF"]
              + [f"dA{n}" for n in range(NST)] + ["dBr0", "dBr1"]
              + [f"Br{n}" for n in range(NST)] + [f"Cr{n}" for n in range(NST)])
    TPG = min(4, NTC)
    NTG = NTC // TPG
    for lt in range(2):
        for kd in range(KD):
            S.dma("sync", ygt[:, kd * Lc:(kd + 1) * Lc],
                  yg_lt[lt].ap()[kd, :, :],
                  reads=(f"yg{lt}", "ygt", "p4_dead"), writes=("ygt",),
                  tag=f"o_y{lt}_{kd}")
        for tg in range(NTG):
            for hc in range(CC):
                par = hc % 2
                ow = (owb0, owb1)[par]
                owk = f"owb{par}"
                for kg in range(KG):
                    S.dma("sync", ow[:, :], woT_sp.ap()[hc, kg, :, :],
                          reads=("woT_sp", owk, "p4_dead"), writes=(owk,),
                          tag=f"o_w{lt}_{tg}_{hc}_{kg}")

                    def mm_o(e, ow=ow, kg=kg, tg=tg, par=par):
                        last = None
                        for ki in range(8):
                            kd = kg * 8 + ki
                            for t2 in range(TPG):
                                tc = tg * TPG + t2
                                last = e.matmul(
                                    bank(par * 4 + t2, width=NCC),
                                    ygt[:, kd * Lc + tc * 128:
                                        kd * Lc + (tc + 1) * 128],
                                    ow[:, ki * NCC:(ki + 1) * NCC],
                                    start=(kd == 0), stop=(kd == KD - 1))
                        return last
                    S.add("tensor", mm_o, reads=(owk, "ygt"),
                          writes=tuple(f"pb{par * 4 + t2}"
                                       for t2 in range(TPG)) + (owk,),
                          tag=f"o_mm{lt}_{tg}_{hc}_{kg}")
                for t2 in range(TPG):
                    tc = tg * TPG + t2
                    row0 = lt * Lc + tc * 128
                    xf = (xfb0, xfb1)[t2 % 2]
                    ff = (ffb0, ffb1)[t2 % 2]
                    xk2 = f"xfb{t2 % 2}"
                    fk2 = f"ffb{t2 % 2}"
                    S.dma("scalar", xf[:, 0:NCC],
                          x_res.ap()[row0:row0 + 128, hc * NCC:(hc + 1) * NCC],
                          reads=(xk2, "p4_dead"), writes=(xk2,),
                          tag=f"o_x{lt}_{tg}_{hc}_{t2}")
                    S.dma("scalar", ff[:, 0:NCC],
                          ffn_o.ap()[row0:row0 + 128, hc * NCC:(hc + 1) * NCC],
                          reads=("ffn_o", fk2, "p4_dead"), writes=(fk2,),
                          tag=f"o_f{lt}_{tg}_{hc}_{t2}")
                    S.add("vector", lambda e, t2=t2, hc=hc, par=par, xf=xf:
                          e.scalar_tensor_tensor(
                              osts[t2][:, hc * NCC:(hc + 1) * NCC],
                              bank(par * 4 + t2, width=NCC), 1.0,
                              xf[:, 0:NCC], ALU.mult, ALU.add),
                          reads=(f"pb{par * 4 + t2}", xk2, f"ost{t2}",
                                 "p4_dead"),
                          writes=(f"ost{t2}", f"pb{par * 4 + t2}"),
                          tag=f"o_ev{lt}_{tg}_{hc}_{t2}")
                    S.add("vector", lambda e, t2=t2, hc=hc, ff=ff:
                          e.tensor_tensor(
                              osts[t2][:, hc * NCC:(hc + 1) * NCC],
                              osts[t2][:, hc * NCC:(hc + 1) * NCC],
                              ff[:, 0:NCC], ALU.add),
                          reads=(fk2, f"ost{t2}"), writes=(f"ost{t2}",),
                          tag=f"o_ev2{lt}_{tg}_{hc}_{t2}")
            for t2 in range(TPG):
                tc = tg * TPG + t2
                row0 = lt * Lc + tc * 128
                S.dma("sync", out_t.ap()[row0:row0 + 128, :], osts[t2][:, 0:HH],
                      reads=(f"ost{t2}",), writes=("out_t",),
                      tag=f"o_st{lt}_{tg}_{t2}")

    S.add("sync", lambda e: None, reads=("out_t",), tag="join")

    ctx = contextlib.ExitStack()
    S.emit_all(nc, ctx)
    nc.finalize()
    return nc


def make_in_maps(cfg, inputs):
    L, H, DIN, NST, KCV, R = (cfg[k] for k in ("L", "H", "DIN", "NST", "KC", "R"))
    DH, HH, L2 = DIN // 2, H // 2, L // 2
    DC, KR, KH = DH // 128, R // 128, H // 128
    f32 = np.float32
    hid = np.asarray(inputs["L0_attn_hidden_states"], dtype=f32)
    w_in_full = np.asarray(inputs["L0_attn_in_proj_w"], dtype=f32)
    conv_w = np.asarray(inputs["L0_attn_conv_w"], dtype=f32)
    conv_b = np.asarray(inputs["L0_attn_conv_b"], dtype=f32)
    x_proj_w = np.asarray(inputs["L0_attn_x_proj_w"], dtype=f32)
    dt_proj_w = np.asarray(inputs["L0_attn_dt_proj_w"], dtype=f32)
    dt_bias = np.asarray(inputs["L0_attn_dt_bias"], dtype=f32)
    A_log = np.asarray(inputs["L0_attn_A_log"], dtype=f32)
    D_param = np.asarray(inputs["L0_attn_D_param"], dtype=f32)
    out_proj_w = np.asarray(inputs["L0_attn_out_proj_w"], dtype=f32)
    dt_ln_w = np.asarray(inputs["L0_attn_dt_ln_w"], dtype=f32)
    B_ln_w = np.asarray(inputs["L0_attn_B_ln_w"], dtype=f32)
    C_ln_w = np.asarray(inputs["L0_attn_C_ln_w"], dtype=f32)
    wr = np.asarray(inputs["L0_ffn_weight_real"], dtype=f32)
    fb = np.asarray(inputs["L0_ffn_bias"], dtype=f32)

    nmask = np.zeros((NST, NST * 128), dtype=f32)
    for n in range(NST):
        nmask[n, n * 128:(n + 1) * 128] = 1.0

    def chan_pack(v, j):
        s = v[j * DH:(j + 1) * DH]
        if s.ndim == 1:
            return np.ascontiguousarray(s.reshape(DC, 128).T, dtype=f32)
        return np.ascontiguousarray(
            s.reshape(DC, 128, s.shape[1]).transpose(1, 0, 2).reshape(128, -1),
            dtype=f32)

    in_maps = []
    for c in range(8):
        b, j = c // 2, c % 2
        m = {}
        m["x_ln"] = np.ascontiguousarray(hid[b, j * L2:(j + 1) * L2, :])
        m["x_res"] = np.ascontiguousarray(hid[b, :, j * HH:(j + 1) * HH])
        wi = np.concatenate(
            [w_in_full[:, j * DH:(j + 1) * DH],
             w_in_full[:, DIN + j * DH:DIN + (j + 1) * DH]], axis=1)
        m["w_in"] = np.ascontiguousarray(
            wi.reshape(KH, 128, 2 * DC, 128).transpose(2, 1, 0, 3)
            .reshape(2 * DC, 128, KH * 128))
        m["conv_w"] = chan_pack(conv_w, j)
        m["conv_b"] = chan_pack(conv_b, j)
        m["dt_bias"] = chan_pack(dt_bias, j)
        m["a_log"] = chan_pack(A_log, j)
        m["d_par"] = chan_pack(D_param, j)
        m["x_projT"] = np.ascontiguousarray(x_proj_w[:, j * DH:(j + 1) * DH].T)
        m["dt_projT"] = np.ascontiguousarray(dt_proj_w[j * DH:(j + 1) * DH, :].T)
        m["dt_ln"] = np.ascontiguousarray(dt_ln_w.reshape(KR, 128).T)
        m["b_ln"] = np.ascontiguousarray(B_ln_w.reshape(NST, 1))
        m["c_ln"] = np.ascontiguousarray(C_ln_w.reshape(NST, 1))
        m["nmask"] = nmask
        m["ident"] = np.eye(128, dtype=f32)
        m["woT"] = np.ascontiguousarray(out_proj_w[j * HH:(j + 1) * HH, :].T)
        m["wfT"] = np.ascontiguousarray(wr[j * HH:(j + 1) * HH, :].T)
        m["ffn_bias"] = np.ascontiguousarray(fb[j * HH:(j + 1) * HH].reshape(1, HH))
        in_maps.append(m)
    return in_maps


def assemble(cfg, results):
    L, H = cfg["L"], cfg["H"]
    HH = H // 2
    out = np.empty((4, L, H), dtype=np.float32)
    for c in range(8):
        b, j = c // 2, c % 2
        out[b, :, j * HH:(j + 1) * HH] = results[c]["out"]
    return out


def kernel(**inputs):
    cfg = dict(FULL)
    nc = build_nc(cfg)
    in_maps = make_in_maps(cfg, inputs)
    res = run_bass_kernel_spmd(nc, in_maps, core_ids=list(range(8)))
    return assemble(cfg, res.results)


# version 11 (final)
# speedup vs baseline: 1.1555x; 1.1555x over previous
import contextlib
import numpy as np
import concourse.bass as bass
import concourse.mybir as mybir
from concourse.bass_utils import run_bass_kernel_spmd

F32 = mybir.dt.float32
BF16 = mybir.dt.bfloat16
AF = mybir.ActivationFunctionType
ALU = mybir.AluOpType
AX = mybir.AxisListType

FULL = dict(L=2048, H=4096, DIN=8192, NST=16, KC=4, R=256)

ENGINES = ("tensor", "vector", "scalar", "gpsimd", "sync")
EPS_LN = 1e-5
EPS_RMS = 1e-6
GROUPS = [[0, 1], [2, 3], [4, 5], [6, 7]]


class Sched:

    NLANES = 4

    def __init__(self):
        self.ops = []
        self.lastw = {}
        self.readers = {}
        self.n_cc = 0
        self.dma_count = {}
        self.last_lane = {}

    def add(self, eng, emit, reads=(), writes=(), inc=1, sem=None, tag=""):
        deps = set()
        for t in reads:
            w = self.lastw.get(t)
            if w is not None:
                deps.add(w)
        for t in writes:
            w = self.lastw.get(t)
            if w is not None:
                deps.add(w)
            deps.update(self.readers.get(t, ()))
        i = len(self.ops)
        deps.discard(i)
        if sem is None:
            if inc == 16:
                n = self.dma_count.get(eng, 0)
                self.dma_count[eng] = n + 1
                lane = n % self.NLANES
                sem = (eng, f"d{lane}")
                prev = self.last_lane.get(sem)
                if prev is not None:
                    deps.add(prev)
                self.last_lane[sem] = i
            else:
                sem = (eng, "c")
        self.ops.append(dict(eng=eng, emit=emit, deps=deps, inc=inc, sem=sem,
                             tag=tag, need=False, target=None))
        for t in reads:
            self.readers.setdefault(t, []).append(i)
        for t in writes:
            self.lastw[t] = i
            self.readers[t] = []
        return i

    def dma(self, queue, out_ap, in_ap, reads=(), writes=(), tag="",
            transpose=False):
        def emit(e, _o=out_ap, _i=in_ap, _t=transpose):
            if _t:
                return e.dma_start(out=_o, in_=_i, transpose=True)
            return e.dma_start(out=_o, in_=_i)
        return self.add(queue, emit, reads=reads, writes=writes, inc=16, tag=tag)

    def cc(self, kind, op, ins, outs, reads=(), writes=(), tag="cc"):
        k = ("cc", self.n_cc)
        self.n_cc += 1

        def emit(e, _k=kind, _op=op, _i=ins, _o=outs):
            return e.collective_compute(_k, _op, replica_groups=GROUPS,
                                        ins=_i, outs=_o)
        return self.add("gpsimd", emit, reads=reads, writes=writes, inc=1,
                        sem=k, tag=tag)

    def barrier(self, name, tokens, eng="vector"):
        return self.add(eng, lambda e: e.engine_nop(),
                        writes=(name,) + tuple(tokens), tag=f"bar_{name}")

    def emit_all(self, nc, ctx):
        for o in self.ops:
            for d in o["deps"]:
                self.ops[d]["need"] = True
        for o in self.ops:
            if o["inc"] == 16:
                o["need"] = True
        cum = {}
        for o in self.ops:
            if o["need"]:
                k = o["sem"]
                cum[k] = cum.get(k, 0) + o["inc"]
                o["target"] = cum[k]
        semkeys = sorted(set(o["sem"] for o in self.ops if o["need"]),
                         key=lambda k: str(k))
        sems = {k: ctx.enter_context(nc.semaphore(f"s_{k[0]}_{k[1]}"))
                for k in semkeys}
        per_eng = {e: [] for e in ENGINES}
        for i, o in enumerate(self.ops):
            per_eng[o["eng"]].append(i)
        known = {e: {} for e in ENGINES}

        def mk_stream(ename):
            def stream(eng):
                for i in per_eng[ename]:
                    o = self.ops[i]
                    waits = {}
                    for d in o["deps"]:
                        od = self.ops[d]
                        v = od["target"]
                        assert v is not None, (i, d, od["tag"])
                        k = od["sem"]
                        if waits.get(k, 0) < v:
                            waits[k] = v
                    for k in sorted(waits, key=lambda kk: str(kk)):
                        v = waits[k]
                        if known[ename].get(k, 0) >= v:
                            continue
                        eng.wait_ge(sems[k], v)
                        known[ename][k] = v
                    inst = o["emit"](eng)
                    if o["need"]:
                        assert inst is not None, o["tag"]
                        inst.then_inc(sems[o["sem"]], o["inc"])
            return stream

        with nc.Block() as block:
            block.tensor(mk_stream("tensor"))
            block.vector(mk_stream("vector"))
            block.scalar(mk_stream("scalar"))
            block.gpsimd(mk_stream("gpsimd"))
            block.sync(mk_stream("sync"))


def build_nc(cfg):
    L, H, DIN, NST, KCV, R = (cfg[k] for k in ("L", "H", "DIN", "NST", "KC", "R"))
    DH = DIN // 2
    HH = H // 2
    L2 = L // 2
    Lc = L2
    LQ = L // 4
    KH = H // 128
    DC = DH // 128
    KR = R // 128
    KD = DIN // 128
    NCC = min(512, HH)
    CC = HH // NCC
    NTC = L2 // 128
    SUB = min(512, Lc)
    NSB = Lc // SUB
    E = R + 2 * NST
    NH = NST // 2
    assert R % 128 == 0 and DH % 128 == 0 and H % 256 == 0 and L % 256 == 0
    assert LQ <= 512 and KD % 8 == 0 and CC <= 4 and NSB <= 2

    nc = bass.Bass("TRN2", target_bir_lowering=False)

    def din(name, shape):
        return nc.dram_tensor(name, list(shape), F32, kind="ExternalInput")

    x_ln = din("x_ln", (L2, H))
    x_res = din("x_res", (L, HH))
    w_in = din("w_in", (2 * DC, 128, KH * 128))
    conv_w = din("conv_w", (128, DC * KCV))
    conv_b = din("conv_b", (128, DC))
    dt_bias = din("dt_bias", (128, DC))
    a_log = din("a_log", (128, DC * NST))
    d_par = din("d_par", (128, DC))
    x_projT = din("x_projT", (DH, E))
    dt_projT = din("dt_projT", (R, DH))
    dt_ln = din("dt_ln", (128, KR))
    b_ln = din("b_ln", (NST, 1))
    c_ln = din("c_ln", (NST, 1))
    nmask = din("nmask", (NST, NST * 128))
    ident = din("ident", (128, 128))
    woT = din("woT", (DIN, HH))
    wfT = din("wfT", (H, HH))
    ffn_bias = din("ffn_bias", (1, HH))
    out_t = nc.dram_tensor("out", [L, HH], F32, kind="ExternalOutput")

    CH = 2 if CC >= 2 else 1
    C2 = CC // CH
    KG = KD // 8
    xn_mine = nc.dram_tensor("xn_mine", [L2, H], BF16)
    xn_all = nc.dram_tensor("xn_all", [L, H], BF16)
    xnT_sp = nc.dram_tensor("xnT_sp", [2 * NTC, 128, KH * 128], BF16)
    xc_sp = nc.dram_tensor("xc_sp", [DC, 128, L], BF16)
    sz_sp = nc.dram_tensor("sz_sp", [DC, 128, L], BF16)
    ssmp_b = nc.dram_tensor("ssmp_b", [E, L], F32)
    ssmp_r = nc.dram_tensor("ssmp_r", [E, L], F32)
    dtpT_sp = nc.dram_tensor("dtpT_sp", [KR, 128, DH], BF16)
    alpha_sp = nc.dram_tensor("alpha_sp", [1, HH], F32)
    wsT_sp = nc.dram_tensor("wsT_sp", [CH, KH, 128, C2 * NCC], BF16)
    woT_sp = nc.dram_tensor("woT_sp", [CC, KG, 128, 8 * NCC], BF16)
    y_lt = [nc.dram_tensor(f"y_lt{t}", [DC, 128, Lc], BF16) for t in range(2)]
    yg_lt = [nc.dram_tensor(f"yg_lt{t}", [2 * DC, 128, Lc], BF16) for t in range(2)]
    ffn_o = nc.dram_tensor("ffn_o", [L, HH], F32)

    base0 = nc.sbuf_base
    top = nc.sbuf_top

    def rup(x):
        return (x + 31) // 32 * 32

    def at(name, shape, dtype, offset):
        return nc.alloc_sbuf_tensor_at(name, list(shape), dtype, offset=offset)

    def seq(base, items):
        off = rup(base)
        hs = []
        for nm, shp, dt in items:
            hs.append(at(nm, shp, dt, off))
            off += rup(int(np.prod(shp[1:])) * mybir.dt.size(dt))
        assert off <= top, f"SBUF overflow after {items[-1][0]}: {off} > {top}"
        return hs, off

    (perm_h, perm_end) = seq(base0, [
        ("cw_sb", [128, DC * KCV], F32),
        ("cb_sb", [128, DC], F32),
        ("dtb_sb", [128, DC], F32),
        ("dtln_sb", [128, KR], F32),
        ("d_sb", [128, DC], F32),
        ("ones1f", [1, 128], F32),
        ("ones128", [128, 1], F32),
        ("ones16", [16, 1], F32),
        ("ones16T", [1, 16], F32),
        ("bln_sb", [16, 1], F32),
        ("cln_sb", [16, 1], F32),
        ("nmask_sb", [16, NST * 128], BF16),
        ("ident_bf", [128, 128], BF16),
    ])
    (cw_sb, cb_sb, dtb_sb, dtln_sb, d_sb, ones1f, ones128, ones16, ones16T,
     bln_sb, cln_sb, nmask_sb, ident_bf) = perm_h

    def gsz(items):
        return sum(rup(int(np.prod(shp[1:])) * mybir.dt.size(dt))
                   for _, shp, dt in items)

    P0A_SZ = (3 * rup(H * 4) + rup(64 * 4) + 2 * rup(H * 2) + 2 * rup(HH * 4)
              + 2 * rup(HH * 2) + rup(HH * 4) + rup(DH * 4))
    P4A_SZ = (3 * rup(NST * Lc * 2) + rup(2 * Lc * 2) + 2 * rup(DC * NST * 4)
              + rup(KR * L * 2) + rup(KR * DH * 2))
    A0 = rup(perm_end)
    SZ_A = max(KH * L * 2, KD * Lc * 2, P0A_SZ, P4A_SZ)
    B0 = A0 + SZ_A
    assert B0 < top, f"A slot too large: {B0} > {top}"

    (p0a_h, _) = seq(A0, [
        ("lnb0", [128, H], F32), ("lnb1", [128, H], F32),
        ("sqd", [128, H], F32),
        ("lns", [128, 64], F32),
        ("xnb0", [128, H], BF16), ("xnb1", [128, H], BF16),
        ("wf0", [128, HH], F32), ("wf1", [128, HH], F32),
        ("sgn0", [128, HH], BF16), ("sgn1", [128, HH], BF16),
        ("alpha_f", [1, HH], F32),
        ("wdt0", [128, DH], F32),
    ])
    (lnb0, lnb1, sqd, lns, xnb0, xnb1, wf0, wf1, sgn0, sgn1, alpha_f, wdt0) = p0a_h
    ((xnT,), _) = seq(A0, [("xnT", [128, KH * L], BF16)])
    (p4a_h, _) = seq(A0, [
        ("Brep", [128, NST * Lc], BF16), ("Crep", [128, NST * Lc], BF16),
        ("dAh", [128, NST * Lc], BF16),
        ("dBr", [128, 2 * Lc], BF16),
        ("A_sb", [128, DC * NST], F32),
        ("hstate", [128, DC * NST], F32),
        ("dtrT", [128, KR * L], BF16),
        ("dtpT_sb", [128, KR * DH], BF16),
    ])
    (Brep, Crep, dAh, dBr, A_sb, hstate, dtrT, dtpT_sb) = p4a_h
    ((ygt,), _) = seq(A0, [("ygt", [128, KD * Lc], BF16)])

    (p1b_h, _) = seq(B0, [
        ("wcb0", [128, KH * 128], BF16), ("wcb1", [128, KH * 128], BF16),
        ("wcbf", [128, KH * 128], F32),
        ("stg0", [128, 4 + L], F32), ("stg1", [128, 4 + L], F32),
        ("cacc", [128, L], F32),
        ("xcb0", [128, L], BF16), ("xcb1", [128, L], BF16),
        ("szb0", [128, L], BF16), ("szb1", [128, L], BF16),
    ])
    (wcb0, wcb1, wcbf, stg0, stg1, cacc, xcb0, xcb1, szb0, szb1) = p1b_h
    (p2c_h, _) = seq(B0, [
        ("xpT_sb", [128, DC * E], BF16),
        ("xsb0", [128, L2], BF16), ("xsb1", [128, L2], BF16),
    ])
    (xpT_sb, xsb0, xsb1) = p2c_h
    (p3a_h, _) = seq(A0, [
        ("sq3", [128, L], F32),
        ("rsd_r", [128, L], F32),
        ("arow3", [1, max(L, HH)], F32),
        ("brow3", [1, max(L, HH)], F32),
        ("ssm_dt", [128, KR * L], F32),
        ("ssm_B", [16, L], F32), ("ssm_C", [16, L], F32),
    ])
    (sq3, rsd_r, arow3, brow3, ssm_dt, ssm_B, ssm_C) = p3a_h
    (p4b_h, p4b_end) = seq(B0, [
        ("dtf", [128, Lc], F32), ("esp", [128, Lc], F32),
        ("dtxb", [128, Lc], BF16),
        ("ybf", [128, Lc], BF16),
        ("xcd0", [128, Lc], BF16), ("xcd1", [128, Lc], BF16),
        ("szd0", [128, Lc], BF16), ("szd1", [128, Lc], BF16),
    ])
    (dtf, esp, dtxb, ybf, xcd0, xcd1, szd0, szd1) = p4b_h
    (f1_h, fend) = seq(p4b_end, [
        ("fxn0", [128, KH * 128], BF16), ("fxn1", [128, KH * 128], BF16),
        ("fws0", [128, C2 * NCC], BF16), ("fws1", [128, C2 * NCC], BF16),
        ("fst0", [128, NCC], F32),
        ("alph_r", [128, HH], F32), ("bias_r", [128, HH], F32),
        ("Bn_bf", [16, L], BF16), ("Cn_bf", [16, L], BF16),
        ("rowF", [1, HH], F32),
    ])
    (fxn0, fxn1, fws0, fws1, fst0, alph_r, bias_r, Bn_bf, Cn_bf, rowF) = f1_h
    fxns = (fxn0, fxn1)
    dtf2 = at("dtf2", [128, Lc], F32, fend - rup(HH * 4))
    (p5b_h, _) = seq(B0, [
        ("ost0", [128, HH], F32), ("ost1", [128, HH], F32),
        ("ost2", [128, HH], F32), ("ost3", [128, HH], F32),
        ("xfb0", [128, NCC], F32), ("xfb1", [128, NCC], F32),
        ("ffb0", [128, NCC], F32), ("ffb1", [128, NCC], F32),
        ("owb0", [128, 8 * NCC], BF16), ("owb1", [128, 8 * NCC], BF16),
    ])
    (ost0, ost1, ost2, ost3, xfb0, xfb1, ffb0, ffb1, owb0, owb1) = p5b_h
    osts = (ost0, ost1, ost2, ost3)

    psA = nc.alloc_psum_tensor("psA", [128, 4 * 512], F32)
    psB = nc.alloc_psum_tensor("psB", [128, 4 * 512], F32)

    def bank(i, rows=128, width=512):
        t = psA if i < 4 else psB
        return t[0:rows, (i % 4) * 512:(i % 4) * 512 + width]

    PB = tuple(f"pb{i}" for i in range(8))
    S = Sched()

    NT0 = L2 // 128
    for t in range(NT0):
        lb = (lnb0, lnb1)[t % 2]
        xb = (xnb0, xnb1)[t % 2]
        tok = f"lnb{t % 2}"
        xtok = f"xnb{t % 2}"
        stok = f"lns{t % 2}"
        c0 = 8 * (t % 2)
        sm, sq, mc, rs, nb = (lns[:, c0 + k:c0 + k + 1] for k in range(5))
        S.dma("sync", lb[:, 0:H], x_ln.ap()[t * 128:(t + 1) * 128, :],
              reads=(tok,), writes=(tok,), tag=f"ln_ld{t}")
        S.add("vector", lambda e, lb=lb, sm=sm: e.reduce_sum(sm, lb[:, 0:H],
                                                             axis=AX.X),
              reads=(tok,), writes=(stok,), tag=f"ln_sum{t}")
        S.add("scalar", lambda e, lb=lb, sq=sq: e.activation(
            sqd[:, 0:H], lb[:, 0:H], AF.Square, accum_out=sq),
            reads=(tok, "sqdump"), writes=(stok, "sqdump"), tag=f"ln_sq{t}")
        S.add("vector", lambda e, sm=sm, mc=mc: e.tensor_scalar(
            mc, sm, 1.0 / H, None, ALU.mult),
            reads=(stok,), writes=(stok,), tag=f"ln_st1{t}")
        S.add("vector", lambda e, sq=sq, rs=rs: e.tensor_scalar(
            rs, sq, 1.0 / H, EPS_LN, ALU.mult, ALU.add),
            reads=(stok,), writes=(stok,), tag=f"ln_st2{t}")
        S.add("vector", lambda e, sm=sm, mc=mc: e.tensor_tensor(
            sm, mc, mc, ALU.mult),
            reads=(stok,), writes=(stok,), tag=f"ln_st3{t}")
        S.add("vector", lambda e, rs=rs, sm=sm: e.tensor_tensor(
            rs, rs, sm, ALU.subtract),
            reads=(stok,), writes=(stok,), tag=f"ln_st4{t}")
        S.add("scalar", lambda e, rs=rs: e.activation(rs, rs, AF.Sqrt),
              reads=(stok,), writes=(stok,), tag=f"ln_rt{t}")
        S.add("vector", lambda e, rs=rs: e.reciprocal(rs, rs),
              reads=(stok,), writes=(stok,), tag=f"ln_fi1{t}")
        S.add("vector", lambda e, nb=nb, mc=mc, rs=rs: e.tensor_tensor(
            nb, mc, rs, ALU.mult),
            reads=(stok,), writes=(stok,), tag=f"ln_fi2{t}")
        S.add("vector", lambda e, nb=nb: e.tensor_scalar(
            nb, nb, -1.0, None, ALU.mult),
            reads=(stok,), writes=(stok,), tag=f"ln_fi3{t}")
        S.add("vector", lambda e, lb=lb, xb=xb, rs=rs, nb=nb: e.tensor_scalar(
            xb[:, 0:H], lb[:, 0:H], rs, nb, ALU.mult, ALU.add),
            reads=(tok, stok, xtok), writes=(xtok,), tag=f"ln_no{t}")
        S.dma("sync", xn_mine.ap()[t * 128:(t + 1) * 128, :], xb[:, 0:H],
              reads=(xtok,), writes=("xn_mine",), tag=f"ln_wr{t}")
    S.cc("AllGather", ALU.bypass, [xn_mine.ap().opt()], [xn_all.ap().opt()],
         reads=("xn_mine",), writes=("xn_all",), tag="ag_xn")

    for nm_, dst, src in (("cw", cw_sb, conv_w), ("cb", cb_sb, conv_b),
                          ("dtb", dtb_sb, dt_bias), ("dtln", dtln_sb, dt_ln),
                          ("dpar", d_sb, d_par), ("bln", bln_sb, b_ln),
                          ("cln", cln_sb, c_ln)):
        S.dma("sync", dst[:, :], src.ap(), writes=(nm_,), tag=f"ld_{nm_}")
    S.dma("gpsimd", nmask_sb[:, :], nmask.ap(), writes=("nmask",), tag="ld_nmask")
    S.dma("gpsimd", ident_bf[:, :], ident.ap(), writes=("ident",), tag="ld_ident")
    for nm_, h in (("ones1f", ones1f), ("ones128", ones128),
                   ("ones16", ones16), ("o16T", ones16T)):
        S.add("vector", lambda e, h=h: e.memset(h[:, :], 1.0), writes=(nm_,),
              tag=f"ms_{nm_}")

    for kr in range(KR):
        S.dma("sync", wdt0[:, 0:DH], dt_projT.ap()[kr * 128:(kr + 1) * 128, :],
              reads=("wdt",), writes=("wdt",), tag=f"dtp_ld{kr}")
        S.add("vector", lambda e, kr=kr: e.tensor_scalar(
            wdt0[:, 0:DH], wdt0[:, 0:DH], dtln_sb[:, kr:kr + 1], None, ALU.mult),
            reads=("wdt", "dtln"), writes=("wdt",), tag=f"dtp_f{kr}")
        S.add("vector", lambda e: e.tensor_copy(xnb0[:, 0:DH], wdt0[:, 0:DH]),
              reads=("wdt", "xnb0"), writes=("xnb0",), tag=f"dtp_c{kr}")
        S.dma("sync", dtpT_sp.ap()[kr, :, :], xnb0[:, 0:DH],
              reads=("xnb0",), writes=("dtpT_sp",), tag=f"dtp_wr{kr}")

    for kc in range(KH):
        wb = (wf0, wf1)[kc % 2]
        sg = (sgn0, sgn1)[kc % 2]
        wtok = f"wf{kc % 2}"
        gtok = f"sgn{kc % 2}"
        S.dma("scalar", wb[:, :], wfT.ap()[kc * 128:(kc + 1) * 128, :],
              reads=(wtok,), writes=(wtok,), tag=f"fp_ld{kc}")
        S.add("scalar", lambda e, wb=wb, sg=sg: e.activation(sg[:, :], wb[:, :],
                                                             AF.Sign),
              reads=(wtok, gtok), writes=(gtok,), tag=f"fp_sg{kc}")
        S.add("scalar", lambda e, wb=wb: e.activation(wb[:, :], wb[:, :], AF.Abs),
              reads=(wtok,), writes=(wtok,), tag=f"fp_ab{kc}")

        def mm_alpha(e, wb=wb, kc=kc):
            last = None
            for c in range(CC):
                last = e.matmul(bank(c, rows=1, width=NCC), ones128[:, :],
                                wb[:, c * NCC:(c + 1) * NCC],
                                start=(kc == 0), stop=(kc == KH - 1))
            return last
        S.add("tensor", mm_alpha, reads=(wtok, "ones128"), writes=PB[:CC],
              tag=f"fp_mm{kc}")
        for ch in range(CH):
            S.dma("sync", wsT_sp.ap()[ch, kc, :, :],
                  sg[:, ch * C2 * NCC:(ch + 1) * C2 * NCC],
                  reads=(gtok,), writes=("wsT_sp",), tag=f"fp_wr{kc}_{ch}")

    def ev_alpha(e):
        last = None
        for c in range(CC):
            last = e.tensor_copy(alpha_f[0:1, c * NCC:(c + 1) * NCC],
                                 bank(c, rows=1, width=NCC))
        return last
    S.add("vector", ev_alpha, reads=PB[:CC] + ("alphaF",),
          writes=("alphaF",) + PB[:CC], tag="ev_alpha")
    S.add("vector", lambda e: e.tensor_scalar(
        alpha_f[0:1, :], alpha_f[0:1, :], 1.0 / H, None, ALU.mult),
        reads=("alphaF",), writes=("alphaF",), tag="ev_alpha2")
    S.dma("sync", alpha_sp.ap(), alpha_f[0:1, :], reads=("alphaF",),
          writes=("alpha_sp",), tag="alpha_wr")

    for kd in range(KD):
        kg, ki = kd // 8, kd % 8
        in_ap = woT.ap()[kd * 128:(kd + 1) * 128, :].rearrange(
            "p (cc s) -> p cc s", cc=CC)
        out_ap = woT_sp.ap()[:, kg, :, ki * NCC:(ki + 1) * NCC].rearrange(
            "cc p s -> p cc s")
        S.dma("gpsimd", out_ap, in_ap, writes=("woT_sp",), tag=f"wo_c{kd}")

    S.barrier("p0a_dead", ["lnb0", "lnb1", "xnb0", "xnb1", "sqdump",
                           "lns0", "lns1", "wdt"])
    for kc in range(KH):
        S.dma("sync", xnT[:, kc * L:(kc + 1) * L],
              xn_all.ap()[:, kc * 128:(kc + 1) * 128],
              reads=("xn_all", "p0a_dead"), writes=("xnT",), tag=f"xnT{kc}",
              transpose=True)
    for tc in range(2 * NTC):
        in_ap = xnT.ap().rearrange("p (kc l) -> p kc l", kc=KH)[
            :, :, tc * 128:(tc + 1) * 128]
        S.dma("gpsimd", xnT_sp.ap()[tc, :, :].rearrange(
            "p (kc s) -> p kc s", kc=KH), in_ap,
            reads=("xnT",), writes=("xnT_sp",), tag=f"xnsp{tc}")

    S.add("vector", lambda e: e.memset(stg0[:, 0:4], 0.0),
          reads=("p0a_dead",), writes=("stgp0",), tag="pad0")
    S.add("vector", lambda e: e.memset(stg1[:, 0:4], 0.0),
          reads=("p0a_dead",), writes=("stgp1",), tag="pad1")

    NCB = 2 * DC

    def w_load(c):
        par = c % 2
        wb = (wcb0, wcb1)[par]
        wtok = f"wcb{par}"
        S.dma("sync", wcbf[:, :], w_in.ap()[c, :, :],
              reads=("wcbf", "p0a_dead"), writes=("wcbf",), tag=f"w_inf{c}")
        S.add("scalar", lambda e, wb=wb: e.activation(wb[:, :], wcbf[:, :],
                                                      AF.Copy),
              reads=("wcbf", wtok), writes=(wtok, "wcbf"), tag=f"w_cast{c}")

    w_load(0)
    for c in range(NCB):
        par = c % 2
        wb = (wcb0, wcb1)[par]
        wtok = f"wcb{par}"
        if c + 1 < NCB:
            w_load(c + 1)
        pbt = tuple(f"pb{par * 4 + q}" for q in range(4))

        def mm_ip(e, wb=wb, par=par):
            last = None
            for kc in range(KH):
                for lq in range(4):
                    last = e.matmul(
                        bank(par * 4 + lq, width=LQ),
                        wb[:, kc * 128:(kc + 1) * 128],
                        xnT[:, kc * L + lq * LQ:kc * L + (lq + 1) * LQ],
                        start=(kc == 0), stop=(kc == KH - 1))
            return last
        S.add("tensor", mm_ip, reads=(wtok, "xnT"), writes=pbt, tag=f"ip_mm{c}")
        if c < DC:
            stg = (stg0, stg1)[par]
            xcb = (xcb0, xcb1)[par]
            sgk = f"stg{par}"
            xck = f"xcb{par}"

            def ev_x(e, stg=stg, par=par):
                last = None
                for lq in range(4):
                    last = e.tensor_copy(
                        stg[:, 4 + lq * LQ:4 + (lq + 1) * LQ],
                        bank(par * 4 + lq, width=LQ))
                return last
            S.add("vector", ev_x, reads=pbt + (f"stgp{par}", sgk),
                  writes=(sgk,) + pbt, tag=f"ip_evx{c}")
            S.add("vector", lambda e, stg=stg, dc=c: e.tensor_scalar(
                cacc[:, 0:L], stg[:, 1:1 + L],
                cw_sb[:, dc * KCV:dc * KCV + 1],
                cb_sb[:, dc:dc + 1], ALU.mult, ALU.add),
                reads=(sgk, "cw", "cb", "cacc"), writes=("cacc",),
                tag=f"conv{c}_0")
            for k in range(1, KCV):
                S.add("vector", lambda e, stg=stg, dc=c, k=k:
                      e.scalar_tensor_tensor(
                          cacc[:, 0:L], stg[:, 1 + k:1 + k + L],
                          cw_sb[:, dc * KCV + k:dc * KCV + k + 1],
                          cacc[:, 0:L], ALU.mult, ALU.add),
                      reads=(sgk, "cw", "cacc"),
                      writes=("cacc",) + ((sgk,) if k == KCV - 1 else ()),
                      tag=f"conv{c}_{k}")
            S.add("scalar", lambda e, xcb=xcb: e.activation(
                xcb[:, 0:L], cacc[:, 0:L], AF.Silu),
                reads=("cacc", xck), writes=(xck,), tag=f"si_xc{c}")
            S.dma("sync", xc_sp.ap()[c, :, :], xcb[:, 0:L],
                  reads=(xck,), writes=("xc_sp",), tag=f"xc_wr{c}")
        else:
            dc = c - DC
            szb = (szb0, szb1)[par]
            szk = f"szb{par}"

            def ev_z(e, szb=szb, par=par):
                last = None
                for lq in range(4):
                    last = e.activation(
                        szb[:, lq * LQ:(lq + 1) * LQ],
                        bank(par * 4 + lq, width=LQ), AF.Silu)
                return last
            S.add("scalar", ev_z, reads=pbt + (szk,), writes=(szk,) + pbt,
                  tag=f"ip_evz{c}")
            S.dma("sync", sz_sp.ap()[dc, :, :], szb[:, 0:L],
                  reads=(szk,), writes=("sz_sp",), tag=f"sz_wr{dc}")

    S.barrier("p1w_dead", ["wcb0", "wcb1", "wcbf", "stg0", "stg1", "cacc",
                           "xcb0", "xcb1", "szb0", "szb1", "stgp0", "stgp1"])
    S.barrier("xnT_dead", ["xnT", "xnT_sp"])
    for dc in range(DC):
        S.dma("gpsimd", xpT_sb[:, dc * E:(dc + 1) * E],
              x_projT.ap()[dc * 128:(dc + 1) * 128, :],
              reads=("p1w_dead", "xpT"), writes=("xpT",), tag=f"xp_ld{dc}")
    e_chunks = [(kr * 128, 128, "r", kr) for kr in range(KR)] + \
               [(R, NST, "B", 0), (R + NST, NST, "C", 0)]
    n_e = len(e_chunks)
    assert 2 * n_e <= 8 and L2 // LQ == 2
    for ltp in range(2):
        for dc in range(DC):
            xb = (xsb0, xsb1)[dc % 2]
            xk = f"xsb{dc % 2}"
            S.dma("sync", xb[:, 0:L2], xc_sp.ap()[dc, :, ltp * L2:(ltp + 1) * L2],
                  reads=("xc_sp", xk, "p1w_dead"), writes=(xk,),
                  tag=f"sp_ld{ltp}_{dc}")

            def mm_sp(e, xb=xb, dc=dc):
                last = None
                for ei, (eoff, rows, _, _) in enumerate(e_chunks):
                    for l4 in range(2):
                        last = e.matmul(
                            bank(ei * 2 + l4, rows=rows, width=LQ),
                            xpT_sb[:, dc * E + eoff:dc * E + eoff + rows],
                            xb[:, l4 * LQ:(l4 + 1) * LQ],
                            start=(dc == 0), stop=(dc == DC - 1))
                return last
            S.add("tensor", mm_sp, reads=(xk, "xpT"),
                  writes=tuple(f"pb{i}" for i in range(2 * n_e)) + (xk,),
                  tag=f"sp_mm{ltp}_{dc}")

        def ev_sp(e, ltp=ltp):
            last = None
            for ei, (eoff, rows, kind, kr) in enumerate(e_chunks):
                for l4 in range(2):
                    o = ltp * L2 + l4 * LQ
                    if kind == "r":
                        dst = ssm_dt[:, kr * L + o:kr * L + o + LQ]
                    elif kind == "B":
                        dst = ssm_B[:, o:o + LQ]
                    else:
                        dst = ssm_C[:, o:o + LQ]
                    last = e.tensor_copy(dst, bank(ei * 2 + l4, rows=rows,
                                                   width=LQ))
            return last
        S.add("vector", ev_sp,
              reads=tuple(f"pb{i}" for i in range(2 * n_e)) + ("ssm_sb",
                                                              "xnT_dead"),
              writes=("ssm_sb",) + tuple(f"pb{i}" for i in range(2 * n_e)),
              tag=f"sp_ev{ltp}")
    for kr in range(KR):
        S.dma("sync", ssmp_b.ap()[kr * 128:(kr + 1) * 128, :],
              ssm_dt[:, kr * L:(kr + 1) * L], reads=("ssm_sb",),
              writes=("ssmp_b",), tag=f"sp_wr{kr}")
    S.dma("sync", ssmp_b.ap()[R:R + NST, :], ssm_B[:, 0:L],
          reads=("ssm_sb",), writes=("ssmp_b",), tag="sp_wrB")
    S.dma("sync", ssmp_b.ap()[R + NST:R + 2 * NST, :], ssm_C[:, 0:L],
          reads=("ssm_sb",), writes=("ssmp_b",), tag="sp_wrC")
    S.cc("AllReduce", ALU.add, [ssmp_b.ap().opt()], [ssmp_r.ap().opt()],
         reads=("ssmp_b",), writes=("ssmp_r",), tag="ar_ssm")
    S.barrier("p2x_dead", ["xpT", "xsb0", "xsb1"])

    S.dma("sync", rowF[0:1, :], alpha_sp.ap(), reads=("alpha_sp", "p1w_dead"),
          writes=("rowF",), tag="ld_rowA")

    def mm_ar(e):
        last = None
        for c in range(CC):
            last = e.matmul(bank(c, width=NCC), ones1f[:, :],
                            rowF[0:1, c * NCC:(c + 1) * NCC],
                            start=True, stop=True)
        return last
    S.add("tensor", mm_ar, reads=("rowF", "ones1f"), writes=PB[:CC],
          tag="f_mmar")

    def ev_ar(e):
        last = None
        for c in range(CC):
            last = e.tensor_copy(alph_r[:, c * NCC:(c + 1) * NCC],
                                 bank(c, width=NCC))
        return last
    S.add("vector", ev_ar, reads=PB[:CC] + ("alphR", "p1w_dead"),
          writes=("alphR",) + PB[:CC], tag="f_evar")
    S.dma("sync", rowF[0:1, :], ffn_bias.ap(), reads=("rowF",),
          writes=("rowF",), tag="ld_rowB")

    def mm_br(e):
        last = None
        for c in range(CC):
            last = e.matmul(bank(c, width=NCC), ones1f[:, :],
                            rowF[0:1, c * NCC:(c + 1) * NCC],
                            start=True, stop=True)
        return last
    S.add("tensor", mm_br, reads=("rowF", "ones1f"), writes=PB[:CC],
          tag="f_mmbr")

    def ev_br(e):
        last = None
        for c in range(CC):
            last = e.tensor_copy(bias_r[:, c * NCC:(c + 1) * NCC],
                                 bank(c, width=NCC))
        return last
    S.add("vector", ev_br, reads=PB[:CC] + ("biasR", "p1w_dead"),
          writes=("biasR",) + PB[:CC], tag="f_evbr")

    TPG_F = min(2, NTC)
    NTG_F = NTC // TPG_F

    def ffn_unit(tch, ch, tg):
        for t2 in range(TPG_F):
            tc = tch * NTC + tg * TPG_F + t2
            fx = fxns[t2]
            S.dma("scalar", fx[:, :], xnT_sp.ap()[tc, :, :],
                  reads=("xnT_sp", f"fx{t2}", "p2x_dead"),
                  writes=(f"fx{t2}",), tag=f"f_x{tch}_{ch}_{tg}_{t2}")
        fxtoks = tuple(f"fx{t2}" for t2 in range(TPG_F))
        yield
        for kc in range(KH):
            fw = (fws0, fws1)[kc % 2]
            fwk = f"fws{kc % 2}"
            S.dma("scalar", fw[:, :], wsT_sp.ap()[ch % CH, kc, :, :],
                  reads=("wsT_sp", fwk, "p1w_dead"), writes=(fwk,),
                  tag=f"f_w{tch}_{ch}_{tg}_{kc}")

            def mm_f(e, fw=fw, kc=kc):
                last = None
                for t2 in range(TPG_F):
                    fx = fxns[t2]
                    for c2 in range(C2):
                        last = e.matmul(
                            bank(4 + t2 * C2 + c2, width=NCC),
                            fx[:, kc * 128:(kc + 1) * 128],
                            fw[:, c2 * NCC:(c2 + 1) * NCC],
                            start=(kc == 0), stop=(kc == KH - 1))
                return last
            S.add("tensor", mm_f, reads=(fwk,) + fxtoks,
                  writes=tuple(f"pb{4 + t2 * C2 + c2}"
                               for t2 in range(TPG_F)
                               for c2 in range(C2)) + (fwk,),
                  tag=f"f_mm{tch}_{ch}_{tg}_{kc}")
            yield
        for t2 in range(TPG_F):
            tc = tch * NTC + tg * TPG_F + t2
            for c2 in range(C2):
                c = (ch % CH) * C2 + c2
                S.add("vector", lambda e, t2=t2, c2=c2, c=c:
                      e.tensor_tensor(fst0[:, 0:NCC],
                                      bank(4 + t2 * C2 + c2, width=NCC),
                                      alph_r[:, c * NCC:(c + 1) * NCC],
                                      ALU.mult),
                      reads=(f"pb{4 + t2 * C2 + c2}", "alphR", "fst0"),
                      writes=("fst0", f"pb{4 + t2 * C2 + c2}"),
                      tag=f"f_ev{tch}_{ch}_{tg}_{t2}_{c2}")
                S.add("vector", lambda e, c=c: e.tensor_tensor(
                    fst0[:, 0:NCC], fst0[:, 0:NCC],
                    bias_r[:, c * NCC:(c + 1) * NCC], ALU.add),
                    reads=("biasR", "fst0"), writes=("fst0",),
                    tag=f"f_ev2{tch}_{ch}_{tg}_{t2}_{c2}")
                S.dma("scalar",
                      ffn_o.ap()[tc * 128:(tc + 1) * 128,
                                 c * NCC:(c + 1) * NCC],
                      fst0[:, 0:NCC], reads=("fst0",), writes=("ffn_o",),
                      tag=f"f_st{tch}_{ch}_{tg}_{t2}_{c2}")

    ffn_units = [(tch, ch, tg) for tch in range(2) for ch in range(CH)
                 for tg in range(NTG_F)]
    n_quanta = len(ffn_units) * (2 + KH)
    total_dc = 2 * DC

    def ffn_gen():
        for u in ffn_units:
            yield from ffn_unit(*u)
            yield
    ffn_it = [ffn_gen(), 0]

    def pop_ffn(frac):
        want = int(frac * n_quanta + 1e-9)
        while ffn_it[1] < want:
            try:
                next(ffn_it[0])
            except StopIteration:
                pass
            ffn_it[1] += 1

    pop_ffn(6.0 / n_quanta)

    for kr in range(KR):
        S.dma("sync", ssm_dt[:, kr * L:(kr + 1) * L],
              ssmp_r.ap()[kr * 128:(kr + 1) * 128, :],
              reads=("ssmp_r", "ssm_sb"), writes=("ssm_sb",), tag=f"n_ld{kr}")
    S.dma("sync", ssm_B[:, 0:L], ssmp_r.ap()[R:R + NST, :],
          reads=("ssmp_r",), writes=("ssm_sbB",), tag="n_ldB")
    S.dma("sync", ssm_C[:, 0:L], ssmp_r.ap()[R + NST:R + 2 * NST, :],
          reads=("ssmp_r",), writes=("ssm_sbC",), tag="n_ldC")
    S.dma("sync", A_sb[:, :], a_log.ap(), reads=("xnT_dead", "A"), writes=("A",),
          tag="ld_alog")
    S.add("scalar", lambda e: e.activation(A_sb[:, :], A_sb[:, :], AF.Exp),
          reads=("A",), writes=("A",), tag="a_exp")
    S.add("vector", lambda e: e.tensor_scalar(A_sb[:, :], A_sb[:, :], -1.0,
                                              None, ALU.mult),
          reads=("A",), writes=("A",), tag="a_neg")
    S.add("vector", lambda e: e.memset(hstate[:, :], 0.0),
          reads=("xnT_dead",), writes=("hst",), tag="h_zero")
    for kr in range(KR):
        S.dma("sync", dtpT_sb[:, kr * DH:(kr + 1) * DH], dtpT_sp.ap()[kr, :, :],
              reads=("dtpT_sp", "xnT_dead", "dtpT"), writes=("dtpT",),
              tag=f"dtp_rl{kr}")
    WL = min(512, L)
    NL4 = L // WL
    assert NL4 <= 4
    for kr in range(KR):
        S.add("scalar", lambda e, kr=kr: e.activation(
            sq3[:, 0:L], ssm_dt[:, kr * L:(kr + 1) * L], AF.Square),
            reads=("ssm_sb", "sq3", "p2x_dead"), writes=("sq3",), tag=f"n_sq{kr}")

        def mm_rs(e, kr=kr):
            last = None
            for l4 in range(NL4):
                last = e.matmul(bank(l4, rows=1, width=WL), ones128[:, :],
                                sq3[:, l4 * WL:(l4 + 1) * WL],
                                start=(kr == 0), stop=(kr == KR - 1))
            return last
        S.add("tensor", mm_rs, reads=("sq3", "ones128"), writes=PB[:NL4],
              tag=f"n_mm{kr}")

    def ev_rs(e):
        last = None
        for l4 in range(NL4):
            last = e.tensor_copy(arow3[0:1, l4 * WL:(l4 + 1) * WL],
                                 bank(l4, rows=1, width=WL))
        return last
    S.add("vector", ev_rs, reads=PB[:NL4] + ("arow3", "p2x_dead"),
          writes=("arow3",) + PB[:NL4], tag="n_ev")
    S.add("vector", lambda e: e.tensor_scalar(
        arow3[0:1, 0:L], arow3[0:1, 0:L], 1.0 / R, EPS_RMS, ALU.mult, ALU.add),
        reads=("arow3",), writes=("arow3",), tag="n_ev2")
    S.add("scalar", lambda e: e.activation(arow3[0:1, 0:L], arow3[0:1, 0:L],
                                           AF.Sqrt),
          reads=("arow3",), writes=("arow3",), tag="n_sqrt")
    S.add("vector", lambda e: e.reciprocal(arow3[0:1, 0:L], arow3[0:1, 0:L]),
          reads=("arow3",), writes=("arow3",), tag="n_rec")

    def mm_rrep(e):
        last = None
        for l4 in range(NL4):
            last = e.matmul(bank(l4, width=WL), ones1f[:, :],
                            arow3[0:1, l4 * WL:(l4 + 1) * WL],
                            start=True, stop=True)
        return last
    S.add("tensor", mm_rrep, reads=("arow3", "ones1f"), writes=PB[:NL4],
          tag="n_mmrep")

    def ev_rrep(e):
        last = None
        for l4 in range(NL4):
            last = e.tensor_copy(rsd_r[:, l4 * WL:(l4 + 1) * WL],
                                 bank(l4, width=WL))
        return last
    S.add("vector", ev_rrep, reads=PB[:NL4] + ("rsd", "p2x_dead"),
          writes=("rsd",) + PB[:NL4], tag="n_evrep")
    for kr in range(KR):
        S.add("vector", lambda e, kr=kr: e.tensor_tensor(
            dtrT[:, kr * L:(kr + 1) * L], ssm_dt[:, kr * L:(kr + 1) * L],
            rsd_r[:, 0:L], ALU.mult),
            reads=("ssm_sb", "rsd", "xnT_dead", "dtrT"), writes=("dtrT",),
            tag=f"n_dtrT{kr}")
    for nm_, src, srctok, lncol, lntok, dst, dsttok in (
            ("B", ssm_B, "ssm_sbB", bln_sb, "bln", Bn_bf, "Bn"),
            ("C", ssm_C, "ssm_sbC", cln_sb, "cln", Cn_bf, "Cn")):
        S.add("scalar", lambda e, src=src: e.activation(
            sq3[0:16, 0:L], src[:, 0:L], AF.Square),
            reads=(srctok, "sq3"), writes=("sq3",), tag=f"n_sq{nm_}")

        def mm_ns(e):
            last = None
            for l4 in range(NL4):
                last = e.matmul(bank(l4, rows=1, width=WL), ones16[:, :],
                                sq3[0:16, l4 * WL:(l4 + 1) * WL],
                                start=True, stop=True)
            return last
        S.add("tensor", mm_ns, reads=("sq3", "ones16"), writes=PB[:NL4],
              tag=f"n_nmm{nm_}")

        def ev_ns(e):
            last = None
            for l4 in range(NL4):
                last = e.tensor_copy(brow3[0:1, l4 * WL:(l4 + 1) * WL],
                                     bank(l4, rows=1, width=WL))
            return last
        S.add("vector", ev_ns, reads=PB[:NL4] + ("brow3",),
              writes=("brow3",) + PB[:NL4], tag=f"n_nev{nm_}")
        S.add("vector", lambda e: e.tensor_scalar(
            brow3[0:1, 0:L], brow3[0:1, 0:L], 1.0 / NST, EPS_RMS,
            ALU.mult, ALU.add),
            reads=("brow3",), writes=("brow3",), tag=f"n_nev2{nm_}")
        S.add("scalar", lambda e: e.activation(brow3[0:1, 0:L],
                                               brow3[0:1, 0:L], AF.Sqrt),
              reads=("brow3",), writes=("brow3",), tag=f"n_nsq{nm_}")
        S.add("vector", lambda e: e.reciprocal(brow3[0:1, 0:L],
                                               brow3[0:1, 0:L]),
              reads=("brow3",), writes=("brow3",), tag=f"n_nrc{nm_}")

        def mm_nrep(e):
            last = None
            for l4 in range(NL4):
                last = e.matmul(bank(l4, rows=16, width=WL),
                                ones16T[0:1, :],
                                brow3[0:1, l4 * WL:(l4 + 1) * WL],
                                start=True, stop=True)
            return last
        S.add("tensor", mm_nrep, reads=("brow3", "o16T"), writes=PB[:NL4],
              tag=f"n_nrep{nm_}")

        def ev_bn1(e, src=src):
            last = None
            for l4 in range(NL4):
                sl = slice(l4 * WL, (l4 + 1) * WL)
                last = e.tensor_tensor(sq3[0:16, sl], src[:, sl],
                                       bank(l4, rows=16, width=WL),
                                       ALU.mult)
            return last
        S.add("vector", ev_bn1,
              reads=(srctok, "sq3") + PB[:NL4],
              writes=("sq3",) + PB[:NL4], tag=f"n_bn1{nm_}")
        S.add("vector", lambda e, lncol=lncol, dst=dst: e.tensor_scalar(
            dst[:, 0:L], sq3[0:16, 0:L], lncol[:, 0:1], None, ALU.mult),
            reads=(lntok, "sq3", "p1w_dead", dsttok),
            writes=(dsttok,), tag=f"n_bn2{nm_}")

    S.barrier("ssm_dead", ["ssm_sb", "ssm_sbB", "ssm_sbC"])
    S.barrier("p3x_dead", ["sq3", "arow3", "brow3", "rsd"])

    sched_dc = [(lt, dc) for lt in range(2) for dc in range(DC)]

    def emit_reps(lt):
        for nm_, srctok, rep, reptok in (("B", "Bn", Brep, "Br"),
                                         ("C", "Cn", Crep, "Cr")):
            srcbf = Bn_bf if nm_ == "B" else Cn_bf
            for n in range(NST):
                pbs = tuple(f"pb{2 + sb}" for sb in range(NSB))

                def mm_rep(e, srcbf=srcbf, n=n, lt=lt):
                    last = None
                    for sb in range(NSB):
                        last = e.matmul(
                            bank(2 + sb, width=SUB),
                            nmask_sb[:, n * 128:(n + 1) * 128],
                            srcbf[:, lt * Lc + sb * SUB:
                                  lt * Lc + (sb + 1) * SUB],
                            start=True, stop=True)
                    return last
                S.add("tensor", mm_rep, reads=(srctok, "nmask", "xnT_dead"),
                      writes=pbs, tag=f"r_mm{nm_}{lt}_{n}")

                def ev_rep(e, rep=rep, n=n):
                    last = None
                    for sb in range(NSB):
                        last = e.tensor_copy(
                            rep[:, n * Lc + sb * SUB:n * Lc + (sb + 1) * SUB],
                            bank(2 + sb, width=SUB))
                    return last
                S.add("vector", ev_rep,
                      reads=pbs + (f"{reptok}{n}", "xnT_dead", "p3x_dead",
                                   "ssm_dead"),
                      writes=(f"{reptok}{n}",) + pbs, tag=f"r_ev{nm_}{lt}_{n}")

    def emit_dt(lt, dc):
        par = dc % 2
        dtfc = (dtf, dtf2)[par]

        def mm_dt(e, dc=dc, lt=lt):
            last = None
            for kr in range(KR):
                for sb in range(NSB):
                    last = e.matmul(
                        bank(sb, width=SUB),
                        dtpT_sb[:, kr * DH + dc * 128:
                                kr * DH + (dc + 1) * 128],
                        dtrT[:, kr * L + lt * Lc + sb * SUB:
                             kr * L + lt * Lc + (sb + 1) * SUB],
                        start=(kr == 0), stop=(kr == KR - 1))
            return last
        pbdt = tuple(f"pb{sb}" for sb in range(NSB))
        S.add("tensor", mm_dt, reads=("dtrT", "dtpT"), writes=pbdt,
              tag=f"s_dtmm{lt}_{dc}")

        def sp1(e, dc=dc):
            last = None
            for sb in range(NSB):
                last = e.activation(esp[:, sb * SUB:(sb + 1) * SUB],
                                    bank(sb, width=SUB), AF.Exp,
                                    bias=dtb_sb[:, dc:dc + 1])
            return last
        S.add("scalar", sp1, reads=pbdt + ("dtb", "esp", "p2x_dead"),
              writes=("esp",) + pbdt, tag=f"s_sp1{lt}_{dc}")
        S.add("scalar", lambda e, dtfc=dtfc: e.activation(
            dtfc[:, 0:Lc], esp[:, 0:Lc], AF.Ln, bias=1.0),
            reads=("esp", f"dtf{par}"),
            writes=(f"dtf{par}",) + (("rowF",) if par == 1 else ()),
            tag=f"s_sp2{lt}_{dc}")

    emit_reps(0)
    emit_dt(*sched_dc[0])
    it = [0]
    for idx, (lt, dc) in enumerate(sched_dc):
        par = dc % 2
        dtfc = (dtf, dtf2)[par]
        xcd = (xcd0, xcd1)[par]
        szd = (szd0, szd1)[par]
        xk = f"xcd{par}"
        zk = f"szd{par}"
        S.dma("sync", xcd[:, 0:Lc], xc_sp.ap()[dc, :, lt * Lc:(lt + 1) * Lc],
              reads=("xc_sp", xk, "p2x_dead"), writes=(xk,),
              tag=f"s_xc{lt}_{dc}")
        S.dma("sync", szd[:, 0:Lc], sz_sp.ap()[dc, :, lt * Lc:(lt + 1) * Lc],
              reads=("sz_sp", zk, "p2x_dead"), writes=(zk,),
              tag=f"s_sz{lt}_{dc}")
        S.add("vector", lambda e, xcd=xcd, dtfc=dtfc: e.tensor_tensor(
            dtxb[:, 0:Lc], dtfc[:, 0:Lc], xcd[:, 0:Lc], ALU.mult),
            reads=(f"dtf{par}", xk, "dtx"), writes=("dtx",),
            tag=f"s_dtx{lt}_{dc}")
        for n in range(NST):
            S.add("scalar", lambda e, n=n, dc=dc, dtfc=dtfc: e.activation(
                dAh[:, n * Lc:(n + 1) * Lc], dtfc[:, 0:Lc], AF.Exp,
                scale=A_sb[:, dc * NST + n:dc * NST + n + 1]),
                reads=(f"dtf{par}", "A", f"dA{n}", "ssm_dead"),
                writes=(f"dA{n}",), tag=f"s_dA{lt}_{dc}_{n}")
            S.add("vector", lambda e, n=n: e.tensor_tensor(
                dBr[:, (n % 2) * Lc:(n % 2) * Lc + Lc], dtxb[:, 0:Lc],
                Brep[:, n * Lc:(n + 1) * Lc], ALU.mult),
                reads=("dtx", f"Br{n}", f"dBr{n % 2}"),
                writes=(f"dBr{n % 2}",), tag=f"s_dB{lt}_{dc}_{n}")
            S.add("vector", lambda e, n=n, dc=dc: e.tensor_tensor_scan(
                dAh[:, n * Lc:(n + 1) * Lc], dAh[:, n * Lc:(n + 1) * Lc],
                dBr[:, (n % 2) * Lc:(n % 2) * Lc + Lc],
                hstate[:, dc * NST + n:dc * NST + n + 1],
                ALU.mult, ALU.add),
                reads=(f"dA{n}", f"dBr{n % 2}", "hst"),
                writes=(f"dA{n}",), tag=f"s_sc{lt}_{dc}_{n}")
            S.add("vector", lambda e, n=n: e.tensor_tensor(
                dBr[:, (n % 2) * Lc:(n % 2) * Lc + Lc],
                dAh[:, n * Lc:(n + 1) * Lc],
                Crep[:, n * Lc:(n + 1) * Lc], ALU.mult),
                reads=(f"dA{n}", f"Cr{n}", f"dBr{n % 2}"),
                writes=(f"dBr{n % 2}",), tag=f"s_hC{lt}_{dc}_{n}")

            def mm_ysum1(e, n=n):
                last = None
                for sb in range(NSB):
                    last = e.matmul(
                        bank(2 + sb, width=SUB), ident_bf[:, :],
                        dBr[:, (n % 2) * Lc + sb * SUB:
                            (n % 2) * Lc + (sb + 1) * SUB],
                        start=(n == 0), stop=(n == NST - 1))
                return last
            S.add("tensor", mm_ysum1,
                  reads=(f"dBr{n % 2}", "ident"),
                  writes=("pb2", "pb3"), tag=f"s_ys{lt}_{dc}_{n}")

        if idx + 1 < len(sched_dc):
            emit_dt(*sched_dc[idx + 1])

        def hupd(e, dc=dc):
            hv = dAh.ap().rearrange("p (n l) -> p n l", n=NST)[:, :, Lc - 1]
            return e.tensor_copy(hstate[:, dc * NST:(dc + 1) * NST], hv)
        S.add("vector", hupd, reads=tuple(f"dA{n}" for n in range(NST))
              + ("hst",), writes=("hst",), tag=f"s_hu{lt}_{dc}")
        S.add("vector", lambda e, xcd=xcd, dc=dc: e.scalar_tensor_tensor(
            ybf[:, 0:Lc], xcd[:, 0:Lc], d_sb[:, dc:dc + 1],
            psA[:, 2 * 512:2 * 512 + Lc], ALU.mult, ALU.add),
            reads=("pb2", "pb3", xk, "dpar", "ybf"),
            writes=("ybf", "pb2", "pb3"),
            tag=f"s_yk{lt}_{dc}")
        S.add("vector", lambda e, szd=szd: e.tensor_tensor(
            ybf[:, 0:Lc], ybf[:, 0:Lc], szd[:, 0:Lc], ALU.mult),
            reads=(zk, "ybf"), writes=("ybf",), tag=f"s_ym{lt}_{dc}")
        S.dma("sync", y_lt[lt].ap()[dc, :, :], ybf[:, 0:Lc],
              reads=("ybf",), writes=(f"y_l{lt}",), tag=f"s_yw{lt}_{dc}")
        it[0] += 1
        pop_ffn(min(1.0, 6.0 / n_quanta + it[0] / total_dc))
        if dc == DC - 1:
            S.cc("AllGather", ALU.bypass, [y_lt[lt].ap().opt()],
                 [yg_lt[lt].ap().opt()], reads=(f"y_l{lt}",),
                 writes=(f"yg{lt}",), tag=f"ag_y{lt}")
            if lt == 0:
                emit_reps(1)
    pop_ffn(1.0)
    try:
        next(ffn_it[0])
        raise AssertionError("ffn generator not fully consumed")
    except StopIteration:
        pass

    S.barrier("p4_dead", ["hst", "ybf", "dtx", "dtf", "esp",
                          "xcd0", "xcd1", "szd0", "szd1",
                          "dtrT", "dtpT", "Bn", "Cn", "A", "dpar", "nmask",
                          "fx0", "fx1", "fws0", "fws1", "fst0", "alphR",
                          "biasR", "rowF"]
              + [f"dA{n}" for n in range(NST)] + ["dBr0", "dBr1"]
              + [f"Br{n}" for n in range(NST)] + [f"Cr{n}" for n in range(NST)])
    TPG = min(4, NTC)
    NTG = NTC // TPG
    for lt in range(2):
        for kd in range(KD):
            S.dma("sync", ygt[:, kd * Lc:(kd + 1) * Lc],
                  yg_lt[lt].ap()[kd, :, :],
                  reads=(f"yg{lt}", "ygt", "p4_dead"), writes=("ygt",),
                  tag=f"o_y{lt}_{kd}")
        for tg in range(NTG):
            for hc in range(CC):
                par = hc % 2
                ow = (owb0, owb1)[par]
                owk = f"owb{par}"
                for kg in range(KG):
                    S.dma("sync", ow[:, :], woT_sp.ap()[hc, kg, :, :],
                          reads=("woT_sp", owk, "p4_dead"), writes=(owk,),
                          tag=f"o_w{lt}_{tg}_{hc}_{kg}")

                    def mm_o(e, ow=ow, kg=kg, tg=tg, par=par):
                        last = None
                        for ki in range(8):
                            kd = kg * 8 + ki
                            for t2 in range(TPG):
                                tc = tg * TPG + t2
                                last = e.matmul(
                                    bank(par * 4 + t2, width=NCC),
                                    ygt[:, kd * Lc + tc * 128:
                                        kd * Lc + (tc + 1) * 128],
                                    ow[:, ki * NCC:(ki + 1) * NCC],
                                    start=(kd == 0), stop=(kd == KD - 1))
                        return last
                    S.add("tensor", mm_o, reads=(owk, "ygt"),
                          writes=tuple(f"pb{par * 4 + t2}"
                                       for t2 in range(TPG)) + (owk,),
                          tag=f"o_mm{lt}_{tg}_{hc}_{kg}")
                for t2 in range(TPG):
                    tc = tg * TPG + t2
                    row0 = lt * Lc + tc * 128
                    xf = (xfb0, xfb1)[t2 % 2]
                    ff = (ffb0, ffb1)[t2 % 2]
                    xk2 = f"xfb{t2 % 2}"
                    fk2 = f"ffb{t2 % 2}"
                    S.dma("scalar", xf[:, 0:NCC],
                          x_res.ap()[row0:row0 + 128, hc * NCC:(hc + 1) * NCC],
                          reads=(xk2, "p4_dead"), writes=(xk2,),
                          tag=f"o_x{lt}_{tg}_{hc}_{t2}")
                    S.dma("scalar", ff[:, 0:NCC],
                          ffn_o.ap()[row0:row0 + 128, hc * NCC:(hc + 1) * NCC],
                          reads=("ffn_o", fk2, "p4_dead"), writes=(fk2,),
                          tag=f"o_f{lt}_{tg}_{hc}_{t2}")
                    S.add("vector", lambda e, t2=t2, hc=hc, par=par, xf=xf:
                          e.scalar_tensor_tensor(
                              osts[t2][:, hc * NCC:(hc + 1) * NCC],
                              bank(par * 4 + t2, width=NCC), 1.0,
                              xf[:, 0:NCC], ALU.mult, ALU.add),
                          reads=(f"pb{par * 4 + t2}", xk2, f"ost{t2}",
                                 "p4_dead"),
                          writes=(f"ost{t2}", f"pb{par * 4 + t2}"),
                          tag=f"o_ev{lt}_{tg}_{hc}_{t2}")
                    S.add("vector", lambda e, t2=t2, hc=hc, ff=ff:
                          e.tensor_tensor(
                              osts[t2][:, hc * NCC:(hc + 1) * NCC],
                              osts[t2][:, hc * NCC:(hc + 1) * NCC],
                              ff[:, 0:NCC], ALU.add),
                          reads=(fk2, f"ost{t2}"), writes=(f"ost{t2}",),
                          tag=f"o_ev2{lt}_{tg}_{hc}_{t2}")
            for t2 in range(TPG):
                tc = tg * TPG + t2
                row0 = lt * Lc + tc * 128
                S.dma("sync", out_t.ap()[row0:row0 + 128, :], osts[t2][:, 0:HH],
                      reads=(f"ost{t2}",), writes=("out_t",),
                      tag=f"o_st{lt}_{tg}_{t2}")

    S.add("sync", lambda e: None, reads=("out_t",), tag="join")

    ctx = contextlib.ExitStack()
    S.emit_all(nc, ctx)
    nc.finalize()
    return nc


def make_in_maps(cfg, inputs):
    L, H, DIN, NST, KCV, R = (cfg[k] for k in ("L", "H", "DIN", "NST", "KC", "R"))
    DH, HH, L2 = DIN // 2, H // 2, L // 2
    DC, KR, KH = DH // 128, R // 128, H // 128
    f32 = np.float32
    hid = np.asarray(inputs["L0_attn_hidden_states"], dtype=f32)
    w_in_full = np.asarray(inputs["L0_attn_in_proj_w"], dtype=f32)
    conv_w = np.asarray(inputs["L0_attn_conv_w"], dtype=f32)
    conv_b = np.asarray(inputs["L0_attn_conv_b"], dtype=f32)
    x_proj_w = np.asarray(inputs["L0_attn_x_proj_w"], dtype=f32)
    dt_proj_w = np.asarray(inputs["L0_attn_dt_proj_w"], dtype=f32)
    dt_bias = np.asarray(inputs["L0_attn_dt_bias"], dtype=f32)
    A_log = np.asarray(inputs["L0_attn_A_log"], dtype=f32)
    D_param = np.asarray(inputs["L0_attn_D_param"], dtype=f32)
    out_proj_w = np.asarray(inputs["L0_attn_out_proj_w"], dtype=f32)
    dt_ln_w = np.asarray(inputs["L0_attn_dt_ln_w"], dtype=f32)
    B_ln_w = np.asarray(inputs["L0_attn_B_ln_w"], dtype=f32)
    C_ln_w = np.asarray(inputs["L0_attn_C_ln_w"], dtype=f32)
    wr = np.asarray(inputs["L0_ffn_weight_real"], dtype=f32)
    fb = np.asarray(inputs["L0_ffn_bias"], dtype=f32)

    nmask = np.zeros((NST, NST * 128), dtype=f32)
    for n in range(NST):
        nmask[n, n * 128:(n + 1) * 128] = 1.0

    def chan_pack(v, j):
        s = v[j * DH:(j + 1) * DH]
        if s.ndim == 1:
            return np.ascontiguousarray(s.reshape(DC, 128).T, dtype=f32)
        return np.ascontiguousarray(
            s.reshape(DC, 128, s.shape[1]).transpose(1, 0, 2).reshape(128, -1),
            dtype=f32)

    in_maps = []
    for c in range(8):
        b, j = c // 2, c % 2
        m = {}
        m["x_ln"] = np.ascontiguousarray(hid[b, j * L2:(j + 1) * L2, :])
        m["x_res"] = np.ascontiguousarray(hid[b, :, j * HH:(j + 1) * HH])
        wi = np.concatenate(
            [w_in_full[:, j * DH:(j + 1) * DH],
             w_in_full[:, DIN + j * DH:DIN + (j + 1) * DH]], axis=1)
        m["w_in"] = np.ascontiguousarray(
            wi.reshape(KH, 128, 2 * DC, 128).transpose(2, 1, 0, 3)
            .reshape(2 * DC, 128, KH * 128))
        m["conv_w"] = chan_pack(conv_w, j)
        m["conv_b"] = chan_pack(conv_b, j)
        m["dt_bias"] = chan_pack(dt_bias, j)
        m["a_log"] = chan_pack(A_log, j)
        m["d_par"] = chan_pack(D_param, j)
        m["x_projT"] = np.ascontiguousarray(x_proj_w[:, j * DH:(j + 1) * DH].T)
        m["dt_projT"] = np.ascontiguousarray(dt_proj_w[j * DH:(j + 1) * DH, :].T)
        m["dt_ln"] = np.ascontiguousarray(dt_ln_w.reshape(KR, 128).T)
        m["b_ln"] = np.ascontiguousarray(B_ln_w.reshape(NST, 1))
        m["c_ln"] = np.ascontiguousarray(C_ln_w.reshape(NST, 1))
        m["nmask"] = nmask
        m["ident"] = np.eye(128, dtype=f32)
        m["woT"] = np.ascontiguousarray(out_proj_w[j * HH:(j + 1) * HH, :].T)
        m["wfT"] = np.ascontiguousarray(wr[j * HH:(j + 1) * HH, :].T)
        m["ffn_bias"] = np.ascontiguousarray(fb[j * HH:(j + 1) * HH].reshape(1, HH))
        in_maps.append(m)
    return in_maps


def assemble(cfg, results):
    L, H = cfg["L"], cfg["H"]
    HH = H // 2
    out = np.empty((4, L, H), dtype=np.float32)
    for c in range(8):
        b, j = c // 2, c % 2
        out[b, :, j * HH:(j + 1) * HH] = results[c]["out"]
    return out


def kernel(**inputs):
    cfg = dict(FULL)
    nc = build_nc(cfg)
    in_maps = make_in_maps(cfg, inputs)
    res = run_bass_kernel_spmd(nc, in_maps, core_ids=list(range(8)))
    return assemble(cfg, res.results)
